# version 19
# baseline (speedup 1.0000x reference)
import contextlib
import numpy as np

import concourse.bass as bass
import concourse.mybir as mybir
from concourse.bass_utils import run_bass_kernel_spmd

F32 = mybir.dt.float32
BF16 = mybir.dt.bfloat16
ALU = mybir.AluOpType
AF = mybir.ActivationFunctionType
AX = mybir.AxisListType

FULL_CFG = dict(DM=2048, H=1408, E=8, TCORE=2048, G=512, HALO=256)
DEBUG_INTERNALS = False
LN_EPS = 1e-5
GATE_EPS = 1e-20


def derive(cfg):
    d = dict(cfg)
    d["NF"] = cfg["DM"] // 128
    d["NHM"] = cfg["H"] // 128
    d["TCIN"] = cfg["TCORE"] + cfg["HALO"]
    G = cfg["G"]
    d["GIN"] = [(0, cfg["HALO"])] + [(cfg["HALO"] + G * k, G)
                                     for k in range(cfg["TCORE"] // G)]
    d["GOWN"] = [(cfg["HALO"] + G * k, G) for k in range(cfg["TCORE"] // G)]
    d["G0"] = [(G * k, G) for k in range(cfg["TCORE"] // G)]
    return d


class B:
    ENGS = ("tensor", "vector", "scalar", "gpsimd", "sync")
    AUTO = {"vector": "msV", "scalar": "msA", "gpsimd": "msP"}

    def __init__(self, nc, st, cfg):
        self.nc, self.st, self.cfg = nc, st, cfg
        self.q = {e: [] for e in self.ENGS}
        self.S, self.C = {}, {}
        self.t = {}
        self.d = {}
        self.qeng = {}
        self.last_auto = {}
        self.barn = 0
        self.wc = {}

    def sem(self, name):
        if name not in self.S:
            self.S[name] = self.st.enter_context(self.nc.semaphore(name))
            self.C[name] = 0
        return self.S[name]

    def op(self, eng, meth, *args, inc=None, n=1, **kw):
        if eng in self.AUTO:
            if eng in self.last_auto:
                self.wait(eng, *self.last_auto[eng])
            name = inc if inc is not None else self.AUTO[eng]
            s = self.sem(name)
            self.C[name] += n
            cnt = self.C[name]
            self.last_auto[eng] = (name, cnt)
            self.q[eng].append(
                lambda e, m=meth, a=args, k=kw, s=s, n=n: getattr(e, m)(*a, **k).then_inc(s, n))
            return cnt
        cnt = None
        if inc is not None:
            s = self.sem(inc)
            self.C[inc] += n
            cnt = self.C[inc]
            self.q[eng].append(
                lambda e, m=meth, a=args, k=kw, s=s, n=n: getattr(e, m)(*a, **k).then_inc(s, n))
        else:
            self.q[eng].append(lambda e, m=meth, a=args, k=kw: getattr(e, m)(*a, **k))
        return cnt

    def wait(self, eng, name, val):
        if val is None or val <= 0:
            return
        s = self.sem(name)
        self.q[eng].append(lambda e, s=s, v=val: e.wait_ge(s, v))

    def dma(self, eng, qname, out, in_, **kw):
        if eng in self.AUTO and eng in self.last_auto:
            self.wait(eng, *self.last_auto[eng])
        s = self.sem(qname)
        self.C[qname] += 16
        cnt = self.C[qname]
        self.q[eng].append(
            lambda e, o=out, i=in_, s=s, k=kw: e.dma_start(out=o, in_=i, **k).then_inc(s, 16))
        self.qeng[qname] = eng
        return cnt

    def inc(self, eng, name, n=1):
        s = self.sem(name)
        self.C[name] += n
        cnt = self.C[name]
        self.q[eng].append(lambda e, s=s, n=n: e.sem_inc(s, n))
        return cnt

    def cur(self, name):
        return self.C.get(name, 0)

    def drain(self, eng, qnames):
        for qn in qnames:
            self.wait(eng, qn, self.cur(qn))

    def barrier(self):
        self.barn += 1
        for qn, eng in self.qeng.items():
            self.wait(eng, qn, self.cur(qn))
        for eng in self.ENGS:
            if eng in self.AUTO and eng in self.last_auto:
                self.wait(eng, *self.last_auto[eng])
            if eng == "tensor":
                self.q[eng].append(lambda e: e.drain())
            self.inc(eng, "bar_" + eng)
        for eng in self.ENGS:
            for o in self.ENGS:
                if o != eng:
                    self.wait(eng, "bar_" + o, self.barn)

    def sb(self, st, name, shape, dt):
        return st.enter_context(self.nc.sbuf_tensor(name, shape, dt))

    def ps(self, st, name, shape, dt=F32):
        return st.enter_context(self.nc.psum_tensor(name, shape, dt))


def fm(dram, f0, nf, t0, nt):
    return dram.ap().rearrange("f p t -> p f t")[:, f0:f0 + nf, t0:t0 + nt]


def row3(sb_ap):
    return sb_ap.rearrange("p (o t) -> p o t", o=1)


def emit_ln_group(bd, cfg, get_x, nt, sc, prev):
    NF, DM = cfg["NF"], cfg["DM"]
    (p_sum, p_sumsq, p_ab, sqt, rows) = sc
    mu, var, alf, bet = [r[0:1, 0:nt] for r in rows]
    p_s = p_sum[0:1, 0:nt]
    p_sq = p_sumsq[0:1, 0:nt]
    bd.wait("scalar", "msV", prev.get("sq_free", 0))
    bd.wait("scalar", "msP", prev.get("sq_freeP", 0))
    bd.wait("tensor", "msV", prev.get("stats_free", 0))
    sq_mm = [0, 0]
    for f in range(NF):
        xap = get_x(f)
        bd.wait("scalar", "msT", sq_mm[f % 2])
        ca = bd.op("scalar", "activation", sqt[f % 2][:, 0:nt], xap, AF.Square,
                   inc="msA")
        bd.op("tensor", "matmul", p_s, bd.t["ones_col"][:], xap,
              start=(f == 0), stop=(f == NF - 1))
        bd.wait("tensor", "msA", ca)
        sq_mm[f % 2] = bd.op("tensor", "matmul", p_sq, bd.t["ones_col"][:],
                             sqt[f % 2][:, 0:nt], start=(f == 0), stop=(f == NF - 1),
                             inc="msT")
    cstat = bd.C["msT"]
    bd.wait("vector", "msT", cstat)
    bd.op("vector", "tensor_scalar_mul", mu, p_s, 1.0 / DM)
    bd.op("vector", "tensor_scalar", var, p_sq, 1.0 / DM, LN_EPS,
          ALU.mult, ALU.add)
    bd.op("vector", "scalar_tensor_tensor", alf, mu, -1.0, mu,
          ALU.mult, ALU.mult)
    cvar = bd.op("vector", "tensor_tensor", var, var, alf, ALU.add, inc="msV")
    bd.wait("scalar", "msV", cvar)
    bd.op("scalar", "activation", var, var, AF.Ln)
    ca = bd.op("scalar", "activation", alf, var, AF.Exp, scale=-0.5, inc="msA")
    bd.wait("vector", "msA", ca)
    cbet = bd.op("vector", "scalar_tensor_tensor", bet, mu, -1.0, alf,
                 ALU.mult, ALU.mult, inc="msV")
    bd.wait("tensor", "msV", cbet)
    bd.wait("tensor", "msV", prev.get("pab_free", 0))
    bd.op("tensor", "matmul", p_ab[0][:, 0:nt], bd.t["ones_row"][:], alf,
          start=True, stop=True)
    cbc = bd.op("tensor", "matmul", p_ab[1][:, 0:nt], bd.t["ones_row"][:], bet,
                start=True, stop=True, inc="msT")
    bd.wait("vector", "msT", cbc)
    return cbc, cvar


def emit_ffn(bd, cfg, src, tag, groups, TL, src_ready_q=(), final_merge=False,
             cache=False):
    DM, H, E, G = cfg["DM"], cfg["H"], cfg["E"], cfg["G"]
    NF, NHM = cfg["NF"], cfg["NHM"]
    NGR = len(groups)
    stx = contextlib.ExitStack()
    xf = bd.sb(stx, tag + "xf", [128, NF * TL], BF16)

    if cache:
        cstf = [bd.sb(stx, "cst%d" % i, [128, 2 * 128], F32) for i in range(2)]
        cstb = [bd.sb(stx, "csb%d" % i, [128, 2 * 128], BF16) for i in range(2)]
        cq = []
        for wn, cn in (("wkT", "wkc"), ("wrT", "wrc"), ("wvT", "wvc"),
                       ("woT", "woc")):
            sap = bd.d[wn].ap().rearrange("(kc q) d -> q kc d", q=128)
            dap = bd.d[cn].ap().rearrange("(kc q) d -> q kc d", q=128)
            for mo in range(NF):
                for kc0 in range(0, NF, 2):
                    cq.append((cn, mo, kc0, sap, dap))
        cst_free = [0, 0]
        cni = [0]

        def pop_cache(nmax):
            for _ in range(min(nmax, len(cq) - cni[0])):
                cn, mo, kc0, sap, dap = cq[cni[0]]
                sl = cni[0] % 2
                cni[0] += 1
                bd.wait("sync", "msA", cst_free[sl])
                cw = bd.dma("sync", "cqf%d" % sl,
                            cstf[sl][:].rearrange("p (kc d) -> p kc d", d=128),
                            sap[:, kc0:kc0 + 2, mo * 128:(mo + 1) * 128])
                bd.wait("scalar", "cqf%d" % sl, cw)
                bd.wait("scalar", "cqo%d" % sl, bd.cur("cqo%d" % sl))
                cst_free[sl] = bd.op("scalar", "copy", cstb[sl][:], cstf[sl][:])
                bd.dma("scalar", "cqo%d" % sl,
                       dap[:, kc0:kc0 + 2, mo * 128:(mo + 1) * 128],
                       cstb[sl][:].rearrange("p (kc d) -> p kc d", d=128))
                if kc0 + 2 >= NF:
                    bd.wc[(cn, mo)] = (bd.cur("cqo0"), bd.cur("cqo1"))
    else:
        def pop_cache(nmax):
            pass

    with contextlib.ExitStack() as st:
        xin = [bd.sb(st, tag + "xin%d" % i, [128, NF * G], F32) for i in range(2)]
        sqt = [bd.sb(st, tag + "sq%d" % i, [128, G], F32) for i in range(2)]
        tfp = [bd.sb(st, tag + "tfp%d" % i, [128, G], F32) for i in range(2)]
        rows = [bd.sb(st, tag + "row%d" % i, [1, G], F32) for i in range(4)]
        r8sb = bd.sb(st, tag + "r8sb", [8, G], F32)
        gtm = bd.sb(st, tag + "gtm", [128, 8], F32)
        gx = [bd.sb(st, tag + "gx%d" % i, [128, 8], F32) for i in range(3)]
        gcol = [bd.sb(st, tag + "gc%d" % i, [128, 1], F32) for i in range(4)]
        gem = bd.sb(st, tag + "gem", [8, G], F32)
        p_sum = bd.ps(st, tag + "p_sum", [1, G])
        p_sumsq = bd.ps(st, tag + "p_sumsq", [1, G])
        p_ab = [bd.ps(st, tag + "p_ab%d" % i, [128, G]) for i in range(2)]
        p_r8 = bd.ps(st, tag + "p_r8", [8, G])
        p_tp = bd.ps(st, tag + "p_tp", [128, 8])
        p_gem = bd.ps(st, tag + "p_gem", [8, 128])

        bd.drain("sync", src_ready_q)
        xin_done = [0, 0]
        prev = {}
        tfp_T = [0, 0]
        tfp_P = [0, 0]
        qg_done = 0
        ptp_free = [0]
        for gi, (t0, nt) in enumerate(groups):
            slot = gi % 2
            nchunks = nt // 128
            bd.wait("sync", "msT", xin_done[slot])
            cx = bd.dma("sync", tag + "qx%d" % slot,
                        row3(xin[slot][:, 0:NF * nt]).rearrange(
                            "p o (f t) -> p (o f) t", t=nt),
                        fm(src, 0, NF, t0, nt))
            bd.wait("scalar", tag + "qx%d" % slot, cx)
            bd.wait("tensor", tag + "qx%d" % slot, cx)
            cbc, cvar = emit_ln_group(
                bd, cfg, lambda f, s=slot, w=nt: xin[s][:, f * w:(f + 1) * w], nt,
                (p_sum, p_sumsq, p_ab, sqt, rows), prev)
            for f in range(NF):
                fs = f % 2
                xap = xin[slot][:, f * nt:(f + 1) * nt]
                bd.wait("vector", "msT", tfp_T[fs])
                bd.wait("vector", "msP", tfp_P[fs])
                bd.op("vector", "tensor_tensor", tfp[fs][:, 0:nt], xap,
                      p_ab[0][:, 0:nt], ALU.mult)
                cv = bd.op("vector", "tensor_tensor", tfp[fs][:, 0:nt],
                           tfp[fs][:, 0:nt], p_ab[1][:, 0:nt], ALU.add, inc="msV")
                bd.wait("gpsimd", "msV", cv)
                tfp_P[fs] = bd.op("gpsimd", "tensor_copy",
                                  xf[:, f * TL + t0: f * TL + t0 + nt],
                                  tfp[fs][:, 0:nt], inc="msP")
                bd.wait("tensor", "msV", cv)
                tfp_T[fs] = bd.op("tensor", "matmul", p_r8[0:8, 0:nt],
                                  bd.t["routT_sb"][:, f * E:(f + 1) * E],
                                  tfp[fs][:, 0:nt],
                                  start=(f == 0), stop=(f == NF - 1), inc="msT")
            xin_done[slot] = crt = bd.C["msT"]
            apply_V = bd.C["msV"]
            prev = dict(stats_free=cvar, pab_free=apply_V, sq_free=apply_V)
            bd.wait("vector", "msT", crt)
            cr8 = bd.op("vector", "tensor_copy", r8sb[0:8, 0:nt], p_r8[0:8, 0:nt],
                        inc="msV")
            gem_V = 0
            for cc in range(nchunks):
                bd.wait("tensor", "msV", cr8)
                bd.wait("tensor", "msV", ptp_free[0])
                ct = bd.op("tensor", "transpose", p_tp[:],
                           r8sb[0:8, cc * 128:(cc + 1) * 128],
                           bd.t["ident_sb"][0:8, 0:8], inc="msT")
                bd.wait("vector", "msT", ct)
                ptp_free[0] = bd.op("vector", "tensor_copy", gtm[:], p_tp[:])
                bd.op("vector", "reduce_max", gcol[0][:], gtm[:], axis=AX.X)
                cneg = bd.op("vector", "tensor_scalar_mul", gcol[1][:], gcol[0][:],
                             -1.0, inc="msV")
                bd.wait("scalar", "msV", cneg)
                ca = bd.op("scalar", "activation", gx[0][:], gtm[:], AF.Exp,
                           bias=gcol[1][:, 0:1], accum_out=gcol[2][:, 0:1], inc="msA")
                bd.wait("vector", "msA", ca)
                bd.op("vector", "reciprocal", gcol[3][:], gcol[2][:])
                bd.op("vector", "tensor_scalar_mul", gx[0][:], gx[0][:], gcol[3][:, 0:1])
                bd.op("vector", "reduce_max", gcol[0][:], gx[0][:], axis=AX.X)
                bd.op("vector", "tensor_single_scalar", gx[1][:], gx[0][:],
                      gcol[0][:, 0:1], ALU.is_ge)
                bd.op("vector", "tensor_scalar", gx[1][:], gx[1][:], -1.0, 1.0,
                      ALU.mult, ALU.add)
                bd.op("vector", "tensor_tensor", gx[1][:], gx[0][:], gx[1][:], ALU.mult)
                bd.op("vector", "reduce_max", gcol[1][:], gx[1][:], axis=AX.X)
                bd.op("vector", "tensor_tensor", gcol[2][:], gcol[0][:], gcol[1][:],
                      ALU.add)
                bd.op("vector", "tensor_scalar_add", gcol[2][:], gcol[2][:], GATE_EPS)
                bd.op("vector", "reciprocal", gcol[3][:], gcol[2][:])
                bd.op("vector", "tensor_single_scalar", gx[2][:], gx[0][:],
                      gcol[1][:, 0:1], ALU.is_ge)
                bd.op("vector", "tensor_tensor", gx[2][:], gx[0][:], gx[2][:], ALU.mult)
                cfin = bd.op("vector", "tensor_scalar_mul", gx[2][:], gx[2][:],
                             gcol[3][:, 0:1], inc="msV")
                bd.wait("tensor", "msV", cfin)
                if gem_V:
                    bd.wait("tensor", "msV", gem_V)
                ct2 = bd.op("tensor", "transpose", p_gem[:], gx[2][:],
                            bd.t["ident_sb"][:], inc="msT")
                bd.wait("vector", "msT", ct2)
                if qg_done:
                    bd.wait("vector", tag + "qg", qg_done)
                gem_V = bd.op("vector", "tensor_copy",
                              gem[0:8, cc * 128:(cc + 1) * 128], p_gem[:], inc="msV")
            bd.wait("sync", "msV", gem_V)
            qg_done = bd.dma("sync", tag + "qg",
                             bd.d["gates_d"].ap()[gi * E:(gi + 1) * E, 0:nt],
                             gem[0:8, 0:nt])
            pop_cache(4)
    bd.barrier()

    with contextlib.ExitStack() as st:
        wslab = [bd.sb(st, tag + "ws%d" % i, [128, NF * 128], F32) for i in range(2)]
        wbf_g = [bd.sb(st, tag + "wbg%d" % i, [128, NF * 128], BF16) for i in range(2)]
        wbf_u = [bd.sb(st, tag + "wbu%d" % i, [128, NF * 128], BF16) for i in range(2)]
        wdbf = [bd.sb(st, tag + "wdb%d" % i, [128, NHM * 128], BF16) for i in range(2)]
        hbuf = bd.sb(st, tag + "h", [128, NHM * TL], BF16)
        stage = [bd.sb(st, tag + "stage%d" % i, [128, TL], F32) for i in range(2)]
        gbc = bd.sb(st, tag + "gbc", [128, TL], F32)
        grow = bd.sb(st, tag + "grow", [1, G], F32)
        if final_merge:
            mgacc = [bd.sb(st, tag + "mga0", [128, TL], F32)] * 2
            mgxat = [bd.sb(st, tag + "mgx0", [128, TL], F32)] * 2
        stmp = [bd.sb(st, tag + "st%d" % i, [128, G], F32) for i in range(2)]
        ttmp = [bd.sb(st, tag + "tt%d" % i, [128, G], F32) for i in range(2)]
        p_g = [bd.ps(st, tag + "p_g%d" % i, [128, G]) for i in range(3)]
        p_u = [bd.ps(st, tag + "p_u%d" % i, [128, G]) for i in range(3)]
        p_d = [bd.ps(st, tag + "p_d%d" % i, [128, G]) for i in range(2)]
        p_bc = p_d[0]

        ws_free = [None, None]
        wbg_free = [0, 0]
        wbu_free = [0, 0]
        wd_free = 0
        wdb_free = [0, 0]
        pg_free = [0, 0, 0]
        pu_free = [0, 0, 0]
        pd_free = [0, 0]
        stmp_free = [0, 0]
        ttmp_free = [0, 0]
        stage_free = [0, 0]
        mg_free = [0, 0]
        out_free = [0, 0]
        gbc_free = 0
        grow_free = 0
        pbc_free = 0
        bd.wait("sync", tag + "qg", bd.cur(tag + "qg"))
        for p in range(E + 2):
            routed = p < E
            if routed:
                wg_ap = bd.d["wg"].ap()[p].rearrange("(kc q) h -> q kc h", q=128)
                wu_ap = bd.d["wu"].ap()[p].rearrange("(kc q) h -> q kc h", q=128)
                wd_ap = bd.d["wd"].ap()[p].rearrange("(hc q) d -> q hc d", q=128)
                h0 = 0
            else:
                wg_ap = bd.d["sg"].ap().rearrange("(kc q) h -> q kc h", q=128)
                wu_ap = bd.d["su"].ap().rearrange("(kc q) h -> q kc h", q=128)
                wd_ap = bd.d["sd"].ap().rearrange("(hc q) d -> q hc d", q=128)
                h0 = (p - E) * H
            if routed:
                cgbc = 0
                for gi, (t0, nt) in enumerate(groups):
                    bd.wait("sync", "msT", grow_free)
                    cgr = bd.dma("sync", tag + "qgr", grow[0:1, 0:nt],
                                 bd.d["gates_d"].ap()[gi * E + p: gi * E + p + 1, 0:nt])
                    bd.wait("tensor", tag + "qgr", cgr)
                    bd.wait("tensor", "msP", gbc_free)
                    bd.wait("tensor", "msV", max(pbc_free, pd_free[0]))
                    ct = bd.op("tensor", "matmul", p_bc[:, 0:nt], bd.t["ones_row"][:],
                               grow[0:1, 0:nt], start=True, stop=True, inc="msT")
                    grow_free = ct
                    bd.wait("vector", "msT", ct)
                    cgbc = pbc_free = bd.op(
                        "vector", "tensor_copy", gbc[:, t0:t0 + nt], p_bc[:, 0:nt],
                        inc="msV")
                pd_free[0] = max(pd_free[0], pbc_free)
            def load_gu(mh):
                sl = mh % 2
                if ws_free[sl]:
                    bd.wait("sync", *ws_free[sl])
                cg = bd.dma("sync", tag + "qw%d" % sl,
                            wslab[sl][:].rearrange("p (kc h) -> p kc h", h=128),
                            wg_ap[:, :, h0 + mh * 128: h0 + (mh + 1) * 128])
                bd.wait("vector", tag + "qw%d" % sl, cg)
                bd.wait("vector", "msT", wbg_free[sl])
                cvg = bd.op("vector", "tensor_copy", wbf_g[sl][:], wslab[sl][:])
                bd.wait("sync", "msV", cvg)
                cu = bd.dma("sync", tag + "qw%d" % sl,
                            wslab[sl][:].rearrange("p (kc h) -> p kc h", h=128),
                            wu_ap[:, :, h0 + mh * 128: h0 + (mh + 1) * 128])
                bd.wait("scalar", tag + "qw%d" % sl, cu)
                bd.wait("scalar", "msT", wbu_free[sl])
                cvu = bd.op("scalar", "copy", wbf_u[sl][:], wslab[sl][:])
                ws_free[sl] = ("msA", cvu)
                return cvg, cvu

            pend = load_gu(0)
            for mh in range(NHM):
                sl = mh % 2
                cvg, cvu = pend
                if mh + 1 < NHM:
                    pend = load_gu(mh + 1)
                pop_cache(5)
                for gi, (t0, nt) in enumerate(groups):
                    gp = gi % 3
                    bd.wait("tensor", "msV", cvg)
                    bd.wait("tensor", "msA", pg_free[gp])
                    for kc in range(NF):
                        cmg = bd.op("tensor", "matmul", p_g[gp][:, 0:nt],
                                    wbf_g[sl][:, kc * 128:(kc + 1) * 128],
                                    xf[:, kc * TL + t0: kc * TL + t0 + nt],
                                    start=(kc == 0), stop=(kc == NF - 1),
                                    inc="msT" if kc == NF - 1 else None)
                    bd.wait("tensor", "msA", cvu)
                    bd.wait("tensor", "msV", pu_free[gp])
                    for kc in range(NF):
                        cmu = bd.op("tensor", "matmul", p_u[gp][:, 0:nt],
                                    wbf_u[sl][:, kc * 128:(kc + 1) * 128],
                                    xf[:, kc * TL + t0: kc * TL + t0 + nt],
                                    start=(kc == 0), stop=(kc == NF - 1),
                                    inc="msT" if kc == NF - 1 else None)
                    if gi == NGR - 1:
                        wbg_free[sl] = cmg
                        wbu_free[sl] = cmu
                    ssl = (mh * NGR + gi) % 2
                    bd.wait("scalar", "msT", cmg)
                    bd.wait("scalar", "msV", stmp_free[ssl])
                    ca = bd.op("scalar", "activation", stmp[ssl][:, 0:nt],
                               p_g[gp][:, 0:nt], AF.Silu, inc="msA")
                    pg_free[gp] = ca
                    bd.wait("vector", "msA", ca)
                    bd.wait("vector", "msT", cmu)
                    bd.wait("vector", "msP", ttmp_free[ssl])
                    cv = bd.op("vector", "tensor_tensor", ttmp[ssl][:, 0:nt],
                               stmp[ssl][:, 0:nt], p_u[gp][:, 0:nt], ALU.mult, inc="msV")
                    pu_free[gp] = cv
                    stmp_free[ssl] = cv
                    bd.wait("gpsimd", "msV", cv)
                    hsl = hbuf[:, mh * TL + t0: mh * TL + t0 + nt]
                    if routed:
                        bd.wait("gpsimd", "msV", cgbc)
                        cp = bd.op("gpsimd", "tensor_tensor", hsl, ttmp[ssl][:, 0:nt],
                                   gbc[:, t0:t0 + nt], ALU.mult, inc="msP")
                    else:
                        cp = bd.op("gpsimd", "tensor_copy", hsl, ttmp[ssl][:, 0:nt],
                                   inc="msP")
                    ttmp_free[ssl] = cp
            hdone = bd.C["msP"]
            if routed:
                gbc_free = hdone
            def load_wd(m):
                sl = m % 2
                stg = wslab[sl][:, 0:NHM * 128]
                fr = wd_stage_free[sl]
                if fr:
                    bd.wait("sync", *fr)
                cd = bd.dma("sync", tag + "qd%d" % sl,
                            stg.rearrange("p (hc d) -> p hc d", d=128),
                            wd_ap[:, h0 // 128: h0 // 128 + NHM, m * 128:(m + 1) * 128])
                bd.wait("vector", tag + "qd%d" % sl, cd)
                bd.wait("vector", "msT", wdb_free[sl])
                cvd = bd.op("vector", "tensor_copy", wdbf[sl][:], stg)
                wd_stage_free[sl] = ("msV", cvd)
                if sl == 1:
                    ws_free[0] = ws_free[0]
                return cvd

            wd_stage_free = [ws_free[0], ws_free[1]]
            pend_d = load_wd(0)
            for m in range(NF):
                sl = m % 2
                cvd = pend_d
                if m + 1 < NF:
                    pend_d = load_wd(m + 1)
                last = final_merge and p == E + 1
                if last:
                    bd.wait("sync", "msV", mg_free[0])
                    bd.wait("sync", tag + "qacc%d" % (m % 2), stage_free[m % 2])
                    cacc = bd.dma("sync", tag + "qmg0",
                                  row3(mgacc[0][:]), fm(bd.d["acc_d"], m, 1, 0, TL))
                    cxat = bd.dma("sync", tag + "qmx0",
                                  row3(mgxat[0][:]), fm(src, m, 1, 0, TL))
                for gi, (t0, nt) in enumerate(groups):
                    dp = gi % 2
                    bd.wait("tensor", "msV", cvd)
                    bd.wait("tensor", "msP", hdone)
                    bd.wait("tensor", "msV", pd_free[dp])
                    if routed and dp == 0:
                        bd.wait("tensor", "msV", pbc_free)
                    for hc in range(NHM):
                        cmd = bd.op("tensor", "matmul", p_d[dp][:, 0:nt],
                                    wdbf[sl][:, hc * 128:(hc + 1) * 128],
                                    hbuf[:, hc * TL + t0: hc * TL + t0 + nt],
                                    start=(hc == 0), stop=(hc == NHM - 1),
                                    inc="msT" if hc == NHM - 1 else None)
                    if gi == NGR - 1:
                        wdb_free[sl] = cmd
                    bd.wait("vector", "msT", cmd)
                    if last:
                        bd.wait("vector", tag + "qmg0", cacc)
                        bd.wait("vector", tag + "qmx0", cxat)
                        bd.wait("vector", tag + "qout%d" % (m % 2), out_free[m % 2])
                        bd.op("vector", "tensor_tensor", stage[m % 2][:, t0:t0 + nt],
                              p_d[dp][:, 0:nt], mgacc[0][:, t0:t0 + nt], ALU.add)
                        cv = bd.op("vector", "scalar_tensor_tensor",
                                   stage[m % 2][:, t0:t0 + nt],
                                   stage[m % 2][:, t0:t0 + nt], 0.5,
                                   mgxat[0][:, t0:t0 + nt], ALU.mult, ALU.add,
                                   inc="msV")
                        pd_free[dp] = cv
                    else:
                        if gi == 0:
                            bd.wait("vector", tag + "qacc%d" % (m % 2),
                                    stage_free[m % 2])
                        cv = bd.op("vector", "tensor_copy",
                                   stage[m % 2][:, t0:t0 + nt],
                                   p_d[dp][:, 0:nt], inc="msV")
                        pd_free[dp] = cv
                ws_free[sl] = wd_stage_free[sl]
                if last:
                    mg_free[0] = bd.C["msV"]
                    bd.wait("scalar", "msV", bd.C["msV"])
                    out_free[m % 2] = bd.dma("scalar", tag + "qout%d" % (m % 2),
                                             fm(bd.d["y_out"], m, 1, 0, TL),
                                             row3(stage[m % 2][:]))
                else:
                    bd.wait("gpsimd", "msV", bd.C["msV"])
                    kw = {} if p == 0 else {"accum_op": ALU.add}
                    for c0 in range(0, TL, 2048):
                        cn = min(2048, TL - c0)
                        stage_free[m % 2] = bd.dma(
                            "gpsimd", tag + "qacc%d" % (m % 2),
                            fm(bd.d["acc_d"], m, 1, c0, cn),
                            row3(stage[m % 2][:, c0:c0 + cn]), **kw)
    if cache:
        pop_cache(1 << 30)
    if final_merge:
        bd.wait("scalar", tag + "qout0", bd.cur(tag + "qout0"))
        bd.wait("scalar", tag + "qout1", bd.cur(tag + "qout1"))
        bd.wait("gpsimd", tag + "qout0", bd.cur(tag + "qout0"))
        bd.wait("gpsimd", tag + "qout1", bd.cur(tag + "qout1"))
    bd.barrier()
    stx.close()


def emit_attn(bd, cfg):
    DM, TCORE, G, HALO = cfg["DM"], cfg["TCORE"], cfg["G"], cfg["HALO"]
    NF, TCIN = cfg["NF"], cfg["TCIN"]
    GIN, GOWN = cfg["GIN"], cfg["GOWN"]
    stp = contextlib.ExitStack()
    xproj = bd.sb(stp, "axp", [128, NF * TCIN], BF16)
    sty = contextlib.ExitStack()
    yall = bd.sb(sty, "yall", [128, NF * TCIN], BF16)

    ek_written = {}
    with contextlib.ExitStack() as st:
        xin = [bd.sb(st, "a0xin%d" % i, [128, G], F32) for i in range(2)]
        accin = [bd.sb(st, "a0acc%d" % i, [128, G], F32) for i in range(2)]
        xmt = bd.sb(st, "a0xm", [128, NF * G], F32)
        sqt = [bd.sb(st, "a0sq%d" % i, [128, G], F32) for i in range(2)]
        rows = [bd.sb(st, "a0row%d" % i, [1, G], F32) for i in range(4)]
        NH2 = NF // 2
        wkbf = [bd.sb(st, "a0kw%d" % i, [128, NH2 * 128], BF16) for i in range(2)]
        kstrip = bd.t["ewrep"]
        p_sum = bd.ps(st, "a0p_sum", [1, G])
        p_sumsq = bd.ps(st, "a0p_sumsq", [1, G])
        p_ab = [bd.ps(st, "a0p_ab%d" % i, [128, G]) for i in range(2)]
        p_km = [bd.ps(st, "a0p_km%d" % i, [128, G]) for i in range(2)]
        wkc_ap = bd.d["wkc"].ap().rearrange("(kc q) d -> q kc d", q=128)
        tmk = bd.t["tmk_sb"]
        kwb_free = [0, 0]
        pkm_free = [0, 0]
        kstrip_free = 0
        ki = [0]

        def load_k(i):
            mo = (i // 2) % NF
            half = i % 2
            sl = i % 2
            bd.wait("sync", "msT", kwb_free[sl])
            if i < 2 * NF:
                c0, c1 = bd.wc[("wkc", mo)]
                bd.wait("sync", "cqo0", c0)
                bd.wait("sync", "cqo1", c1)
            cw = bd.dma("sync", "kqw%d" % sl,
                        wkbf[sl][:].rearrange("p (kc d) -> p kc d", d=128),
                        wkc_ap[:, half * NH2:(half + 1) * NH2,
                               mo * 128:(mo + 1) * 128])
            return ("kqw%d" % sl, cw)

        bd.drain("sync", ["f1qacc0", "f1qacc1"])
        prev = {}
        free_T = free_V = free_A = 0
        xin_free = [0, 0]
        xmq = 0
        kpend = load_k(0)
        for (t0, nt) in GIN:
            bd.wait("sync", "msT", free_T)
            bd.wait("sync", "msV", free_V)
            bd.wait("sync", "msA", free_A)
            bd.wait("sync", "a0qxm", xmq)
            for f in range(NF):
                fs = f % 2
                bd.wait("sync", "msV", xin_free[fs])
                cx = bd.dma("sync", "a0qx%d" % fs, row3(xin[fs][:, 0:nt]),
                            fm(bd.d["x_in"], f, 1, t0, nt))
                cac = bd.dma("sync", "a0qa%d" % fs, row3(accin[fs][:, 0:nt]),
                             fm(bd.d["acc_d"], f, 1, t0, nt))
                bd.wait("vector", "a0qx%d" % fs, cx)
                bd.wait("vector", "a0qa%d" % fs, cac)
                xin_free[fs] = bd.op(
                    "vector", "scalar_tensor_tensor", xmt[:, f * nt:(f + 1) * nt],
                    accin[fs][:, 0:nt], 0.5, xin[fs][:, 0:nt],
                    ALU.mult, ALU.add, inc="msV")
            cxm = bd.C["msV"]
            bd.wait("scalar", "msV", cxm)
            xmq = bd.dma("scalar", "a0qxm", fm(bd.d["xmid_d"], 0, NF, t0, nt),
                         row3(xmt[:, 0:NF * nt]).rearrange("p o (f t) -> p (o f) t", t=nt))
            bd.wait("tensor", "msV", cxm)
            cbc, cvar = emit_ln_group(
                bd, cfg, lambda f, w=nt: xmt[:, f * w:(f + 1) * w], nt,
                (p_sum, p_sumsq, p_ab, sqt, rows), prev)
            for f in range(NF):
                bd.op("vector", "tensor_tensor", sqt[f % 2][:, 0:nt],
                      xmt[:, f * nt:(f + 1) * nt], p_ab[0][:, 0:nt], ALU.mult)
                bd.op("vector", "tensor_tensor",
                      yall[:, f * TCIN + t0: f * TCIN + t0 + nt],
                      sqt[f % 2][:, 0:nt], p_ab[1][:, 0:nt], ALU.add,
                      inc="msV" if f % 2 == 1 or f == NF - 1 else None)
            apply_V = bd.C["msV"]
            free_T, free_V, free_A = bd.C["msT"], apply_V, bd.C["msA"]
            bd.wait("gpsimd", "msV", apply_V)
            if t0 == 0:
                s1, s0, w = 1, 0, nt - 1
            else:
                s1, s0, w = t0, t0 - 1, nt
            for f in range(NF):
                b = f * TCIN
                if f % 2 == 0:
                    bd.op("vector", "tensor_sub", sqt[0][:, 0:w],
                          yall[:, b + s1: b + s1 + w], yall[:, b + s0: b + s0 + w])
                    bd.op("vector", "scalar_tensor_tensor",
                          xproj[:, b + s1: b + s1 + w], sqt[0][:, 0:w],
                          tmk[:, f:f + 1], yall[:, b + s0: b + s0 + w],
                          ALU.mult, ALU.add)
                    if t0 == 0:
                        bd.op("vector", "tensor_scalar_mul",
                              xproj[:, b:b + 1], yall[:, b:b + 1], tmk[:, f:f + 1])
                else:
                    bd.op("gpsimd", "tensor_sub", sqt[1][:, 0:w],
                          yall[:, b + s1: b + s1 + w], yall[:, b + s0: b + s0 + w])
                    bd.op("gpsimd", "tensor_mul", sqt[1][:, 0:w], sqt[1][:, 0:w],
                          tmk[:, f:f + 1].broadcast_to([128, w]))
                    bd.op("gpsimd", "tensor_add",
                          xproj[:, b + s1: b + s1 + w], sqt[1][:, 0:w],
                          yall[:, b + s0: b + s0 + w])
                    if t0 == 0:
                        bd.op("gpsimd", "tensor_mul",
                              xproj[:, b:b + 1], yall[:, b:b + 1], tmk[:, f:f + 1])
            cmkV = bd.C["msV"]
            cmkP = bd.C["msP"]
            for mo in range(NF):
                gpk = mo % 2
                bd.wait("tensor", "msV", cmkV)
                bd.wait("tensor", "msP", cmkP)
                bd.wait("tensor", "msA", pkm_free[gpk])
                for half in range(2):
                    ckw = kpend
                    sl = ki[0] % 2
                    ki[0] += 1
                    if ki[0] < len(GIN) * NF * 2:
                        kpend = load_k(ki[0])
                    bd.wait("tensor", *ckw)
                    for kc in range(NH2):
                        cmm = bd.op("tensor", "matmul", p_km[gpk][:, 0:nt],
                                    wkbf[sl][:, kc * 128:(kc + 1) * 128],
                                    xproj[:, (half * NH2 + kc) * TCIN + t0:
                                           (half * NH2 + kc) * TCIN + t0 + nt],
                                    start=(half == 0 and kc == 0),
                                    stop=(half == 1 and kc == NH2 - 1),
                                    inc="msT" if kc == NH2 - 1 else None)
                    kwb_free[sl] = cmm
                bd.wait("scalar", "msT", cmm)
                bd.wait("scalar", "kqs", kstrip_free)
                ca = bd.op("scalar", "activation", kstrip[:, 0:nt],
                           p_km[gpk][:, 0:nt], AF.Exp, inc="msA")
                pkm_free[gpk] = ca
                kstrip_free = bd.dma("scalar", "kqs",
                                     fm(bd.d["ek_d"], mo, 1, t0, nt),
                                     row3(kstrip[:, 0:nt]))
                ek_written[(mo, t0)] = ("kqs", kstrip_free)
            prev = dict(stats_free=cvar, pab_free=apply_V, sq_free=cmkV,
                        sq_freeP=cmkP)
    bd.barrier()

    sr_written = {}
    with contextlib.ExitStack() as st:
        dsc = bd.sb(st, "adsc", [128, TCIN], F32)
        dsc2 = bd.sb(st, "adsc2", [128, TCIN], F32)
        wbf = [bd.sb(st, "awb%d" % i, [128, NF * 128], BF16) for i in range(2)]
        strip = [bd.sb(st, "astr%d" % i, [128, G], F32) for i in range(2)]
        p_mm = [bd.ps(st, "ap_mm%d" % i, [128, G]) for i in range(3)]

        wb_free = [0, 0]
        strip_free = [0, 0]

        def emit_mix(tm_name, free_at):
            bd.wait("vector", "msT", free_at)
            bd.wait("gpsimd", "msT", free_at)
            tm = bd.t[tm_name]
            for f in range(NF):
                y0 = yall[:, f * TCIN: f * TCIN + TCIN]
                if f % 2 == 0:
                    bd.op("vector", "tensor_sub", dsc[:, 0:TCIN - 1],
                          y0[:, 1:TCIN], y0[:, 0:TCIN - 1])
                    bd.op("vector", "scalar_tensor_tensor",
                          xproj[:, f * TCIN + 1: f * TCIN + TCIN],
                          dsc[:, 0:TCIN - 1], tm[:, f:f + 1], y0[:, 0:TCIN - 1],
                          ALU.mult, ALU.add)
                    bd.op("vector", "tensor_scalar_mul",
                          xproj[:, f * TCIN: f * TCIN + 1], y0[:, 0:1], tm[:, f:f + 1])
                else:
                    bd.op("gpsimd", "tensor_sub", dsc2[:, 0:TCIN - 1],
                          y0[:, 1:TCIN], y0[:, 0:TCIN - 1])
                    bd.op("gpsimd", "tensor_mul", dsc2[:, 0:TCIN - 1],
                          dsc2[:, 0:TCIN - 1],
                          tm[:, f:f + 1].broadcast_to([128, TCIN - 1]))
                    bd.op("gpsimd", "tensor_add",
                          xproj[:, f * TCIN + 1: f * TCIN + TCIN],
                          dsc2[:, 0:TCIN - 1], y0[:, 0:TCIN - 1])
                    bd.op("gpsimd", "tensor_mul",
                          xproj[:, f * TCIN: f * TCIN + 1], y0[:, 0:1], tm[:, f:f + 1])
            return (bd.C["msV"], bd.C["msP"])

        last_mm = 0
        pm_free = [None, None, None]
        for wname, tmname, proj in (("wrc", "tmr_sb", "r"),):
            bd.wait("scalar", "msV", bd.C["msV"])
            cmix = emit_mix(tmname, last_mm)
            glist = GOWN
            w_ap = bd.d[wname].ap().rearrange("(kc q) d -> q kc d", q=128)

            def load_a(mo):
                sl = mo % 2
                bd.wait("sync", "msT", wb_free[sl])
                c0, c1 = bd.wc[(wname, mo)]
                bd.wait("sync", "cqo0", c0)
                bd.wait("sync", "cqo1", c1)
                cw = bd.dma("sync", "aqw%d" % sl,
                            wbf[sl][:].rearrange("p (kc d) -> p kc d", d=128),
                            w_ap[:, :, mo * 128:(mo + 1) * 128])
                return ("aqw%d" % sl, cw)

            pend_w = load_a(0)
            for mo in range(NF):
                sl = mo % 2
                cvw = pend_w
                if mo + 1 < NF:
                    pend_w = load_a(mo + 1)
                for gi, (t0, nt) in enumerate(glist):
                    gp = gi % 3
                    ssl = gi % 2
                    bd.wait("tensor", *cvw)
                    bd.wait("tensor", "msV", cmix[0])
                    bd.wait("tensor", "msP", cmix[1])
                    if pm_free[gp]:
                        bd.wait("tensor", *pm_free[gp])
                    for kc in range(NF):
                        cmm = bd.op("tensor", "matmul", p_mm[gp][:, 0:nt],
                                    wbf[sl][:, kc * 128:(kc + 1) * 128],
                                    xproj[:, kc * TCIN + t0: kc * TCIN + t0 + nt],
                                    start=(kc == 0), stop=(kc == NF - 1),
                                    inc="msT" if kc == NF - 1 else None)
                    if gi == len(glist) - 1:
                        wb_free[sl] = cmm
                    qn = "qstr%d" % ssl
                    bd.wait("scalar", "msT", cmm)
                    bd.wait("scalar", qn, strip_free[ssl])
                    ca = bd.op("scalar", "activation", strip[ssl][:, 0:nt],
                               p_mm[gp][:, 0:nt],
                               AF.Exp if proj == "k" else AF.Sigmoid, inc="msA")
                    pm_free[gp] = ("msA", ca)
                    dname = "ek_d" if proj == "k" else "sr_d"
                    strip_free[ssl] = bd.dma(
                        "scalar", qn, fm(bd.d[dname], mo, 1, t0, nt),
                        row3(strip[ssl][:, 0:nt]))
                    if proj == "k":
                        ek_written[(mo, t0)] = (qn, strip_free[ssl])
                    else:
                        sr_written[(mo, t0)] = (qn, strip_free[ssl])
            last_mm = bd.C["msT"]
        bd.wait("scalar", "msV", bd.C["msV"])
        cmixv = emit_mix("tmv_sb", last_mm)
    sty.close()
    bd.barrier()

    with contextlib.ExitStack() as st:
        rwkv = bd.sb(st, "brw", [128, NF * TCORE], BF16)
        wvbf = [bd.sb(st, "vwb%d" % i, [128, NF * 128], BF16) for i in range(2)]
        wobf = [bd.sb(st, "owb%d" % i, [128, NF * 128], BF16) for i in range(2)]
        ekin = [bd.sb(st, "vek%d" % i, [128, G], F32) for i in range(2)]
        srin = [bd.sb(st, "vsr%d" % i, [128, G], F32) for i in range(2)]
        strip = [bd.sb(st, "vstr%d" % i, [128, G], F32) for i in range(2)]
        numo = [bd.sb(st, "bnum%d" % i, [128, G + 1], F32) for i in range(2)]
        deno = [bd.sb(st, "bden%d" % i, [128, G + 1], F32) for i in range(2)]
        bb = [bd.sb(st, "bb%d" % i, [128, G], F32) for i in range(2)]
        bxm = [bd.sb(st, "bxm0", [128, G], F32)] * 2
        bxo = [bd.sb(st, "bxo0", [128, G], F32)] * 2
        p_mm = [bd.ps(st, "bp_mm%d" % i, [128, G]) for i in range(3)]
        p_wo = [bd.ps(st, "bp_wo%d" % i, [128, G]) for i in range(2)]

        wvc_ap = bd.d["wvc"].ap().rearrange("(kc q) d -> q kc d", q=128)
        woc_ap = bd.d["woc"].ap().rearrange("(kc q) d -> q kc d", q=128)

        wvb_free = [None, None]
        wob_free = [0, 0]
        strip_free = [None, None]
        ekin_free = [None, None]
        srin_free = [0, 0]
        pm_free = [None, None, None]
        numo_free = [0, 0]
        crcp_prev = 0
        crw_g = {}
        pm_free2 = [0, 0]
        bxm_free = [0, 0]
        xo_free = [0, 0]

        def load_v(gi, mo):
            sl = mo % 2
            if wvb_free[sl]:
                bd.wait("sync", *wvb_free[sl])
            if gi == 0:
                c0, c1 = bd.wc[("wvc", mo)]
                bd.wait("sync", "cqo0", c0)
                bd.wait("sync", "cqo1", c1)
            cw = bd.dma("sync", "vqw%d" % sl,
                        wvbf[sl][:].rearrange("p (kc d) -> p kc d", d=128),
                        wvc_ap[:, :, mo * 128:(mo + 1) * 128])
            return ("vqw%d" % sl, cw)

        def emit_wo(gi):
            t0o = (gi - 1) * G
            bd.wait("tensor", "msV", crw_g[gi][0])
            bd.wait("tensor", "msP", crw_g[gi][1])
            for co in range(NF):
                sl = co % 2
                if wob_free[sl]:
                    bd.wait("sync", *wob_free[sl])
                else:
                    c0, c1 = bd.wc[("woc", co)]
                    bd.wait("sync", "cqo0", c0)
                    bd.wait("sync", "cqo1", c1)
                cwo = bd.dma("sync", "oqw%d" % sl,
                             wobf[sl][:].rearrange("p (kc d) -> p kc d", d=128),
                             woc_ap[:, :, co * 128:(co + 1) * 128])
                gp2 = co % 2
                xsl = 0
                bd.wait("sync", "msP", bxm_free[xsl])
                cxm = bd.dma("sync", "bqxm%d" % xsl, row3(bxm[xsl][:]),
                             fm(bd.d["xmid_d"], co, 1, t0o + HALO, G))
                bd.wait("tensor", "oqw%d" % sl, cwo)
                bd.wait("tensor", "msA", pm_free2[gp2])
                for kc in range(NF):
                    cmm = bd.op("tensor", "matmul", p_wo[gp2][:],
                                wobf[sl][:, kc * 128:(kc + 1) * 128],
                                rwkv[:, kc * TCORE + t0o: kc * TCORE + t0o + G],
                                start=(kc == 0), stop=(kc == NF - 1),
                                inc="msT" if kc == NF - 1 else None)
                wob_free[sl] = ("msT", cmm)
                bd.wait("scalar", "msT", cmm)
                bd.wait("scalar", "bqxa%d" % xsl, xo_free[xsl])
                bd.wait("scalar", "msP", bxm_free[xsl])
                ca = bd.op("scalar", "copy", bxo[xsl][:], p_wo[gp2][:])
                pm_free2[gp2] = ca
                bd.wait("gpsimd", "msA", ca)
                bd.wait("gpsimd", "bqxm%d" % xsl, cxm)
                cp = bd.op("gpsimd", "tensor_tensor", bxo[xsl][:], bxo[xsl][:],
                           bxm[xsl][:], ALU.add)
                bxm_free[xsl] = cp
                bd.wait("scalar", "msP", cp)
                xo_free[xsl] = bd.dma("scalar", "bqxa%d" % xsl,
                                      fm(bd.d["xattn_d"], co, 1, t0o, G),
                                      row3(bxo[xsl][:]))

        seq = [(gi, mo) for gi in range(len(GIN)) for mo in range(NF)]
        pend_v = load_v(*seq[0])
        for gi, (t0, nt) in enumerate(GIN):
            for mo in range(NF):
                sl = mo % 2
                cvw = pend_v
                idx = gi * NF + mo
                if idx + 1 < len(seq):
                    pend_v = load_v(*seq[idx + 1])
                gp = (gi * NF + mo) % 3
                bd.wait("tensor", *cvw)
                bd.wait("tensor", "msV", cmixv[0])
                bd.wait("tensor", "msP", cmixv[1])
                if pm_free[gp]:
                    bd.wait("tensor", *pm_free[gp])
                for kc in range(NF):
                    cmm = bd.op("tensor", "matmul", p_mm[gp][:, 0:nt],
                                wvbf[sl][:, kc * 128:(kc + 1) * 128],
                                xproj[:, kc * TCIN + t0: kc * TCIN + t0 + nt],
                                start=(kc == 0), stop=(kc == NF - 1),
                                inc="msT" if kc == NF - 1 else None)
                wvb_free[sl] = ("msT", cmm)
                ssl = mo % 2
                eq, ec = ek_written[(mo, t0)]
                bd.wait("sync", eq, ec)
                if ekin_free[ssl]:
                    bd.wait("sync", *ekin_free[ssl])
                cek = bd.dma("sync", "vqe%d" % ssl, row3(ekin[ssl][:, 0:nt]),
                             fm(bd.d["ek_d"], mo, 1, t0, nt))
                bd.wait("vector", "vqe%d" % ssl, cek)
                bd.wait("vector", "msT", cmm)
                if strip_free[ssl]:
                    bd.wait("vector", *strip_free[ssl])
                cv = bd.op("vector", "tensor_tensor", strip[ssl][:, 0:nt],
                           p_mm[gp][:, 0:nt], ekin[ssl][:, 0:nt], ALU.mult)
                pm_free[gp] = ("msV", cv)
                cer = bd.op("vector", "tensor_scalar_mul", bd.t["ewrep"][:, 0:nt],
                            bd.t["ones_g"][:, 0:nt], bd.t["ew"][:, mo:mo + 1])
                if gi == 0:
                    bd.op("vector", "tensor_scalar_mul", bb[0][:, 0:nt],
                          strip[ssl][:, 0:nt], bd.t["hv_sb"][:, 0:1])
                    bd.op("vector", "tensor_tensor_scan", bb[0][:, 0:nt],
                          bd.t["ewrep"][:, 0:nt], bb[0][:, 0:nt],
                          0.0, ALU.mult, ALU.add)
                    bd.op("vector", "tensor_copy",
                          bd.t["numcar"][:, mo:mo + 1], bb[0][:, nt - 1:nt])
                    bd.op("vector", "tensor_scalar_mul", bb[0][:, 0:nt],
                          ekin[ssl][:, 0:nt], bd.t["hv_sb"][:, 0:1])
                    cds = bd.op("vector", "tensor_tensor_scan", bb[0][:, 0:nt],
                                bd.t["ewrep"][:, 0:nt], bb[0][:, 0:nt],
                                0.0, ALU.mult, ALU.add)
                    ekin_free[ssl] = ("msV", cds)
                    strip_free[ssl] = ("msV", cds)
                    cfin = bd.op("vector", "tensor_copy",
                                 bd.t["dencar"][:, mo:mo + 1], bb[0][:, nt - 1:nt])
                    continue
                sq, sc2 = sr_written[(mo, t0)]
                bd.wait("sync", sq, sc2)
                bd.wait("sync", "msV", srin_free[ssl])
                csr = bd.dma("sync", "vqs%d" % ssl, row3(srin[ssl][:]),
                             fm(bd.d["sr_d"], mo, 1, t0, nt))
                pp = idx % 2
                no, de = numo[pp], deno[pp]
                bd.wait("vector", "msP", numo_free[pp])
                bd.op("vector", "tensor_copy", no[:, 0:1],
                      bd.t["numcar"][:, mo:mo + 1])
                cns = bd.op("vector", "tensor_tensor_scan", no[:, 1:G + 1],
                            bd.t["ewrep"][:], strip[ssl][:], no[:, 0:1],
                            ALU.mult, ALU.add)
                bd.op("vector", "tensor_copy", bd.t["numcar"][:, mo:mo + 1],
                      no[:, G:G + 1])
                bd.op("vector", "tensor_copy", de[:, 0:1],
                      bd.t["dencar"][:, mo:mo + 1])
                cds = bd.op("vector", "tensor_tensor_scan", de[:, 1:G + 1],
                            bd.t["ewrep"][:], ekin[ssl][:], de[:, 0:1],
                            ALU.mult, ALU.add)
                bd.op("vector", "tensor_copy", bd.t["dencar"][:, mo:mo + 1],
                      de[:, G:G + 1])
                bd.wait("gpsimd", "msV", cds)
                bd.wait("gpsimd", "msA", crcp_prev)
                bd.op("gpsimd", "tensor_mul", bb[0][:], ekin[ssl][:],
                      bd.t["eu"][:, mo:mo + 1].broadcast_to([128, G]))
                cp1 = bd.op("gpsimd", "tensor_mul", bb[1][:], strip[ssl][:],
                            bd.t["eu"][:, mo:mo + 1].broadcast_to([128, G]))
                ekin_free[ssl] = ("msP", cp1)
                strip_free[ssl] = ("msP", cp1)
                bd.op("gpsimd", "tensor_add", no[:, 0:G], no[:, 0:G], bb[1][:])
                cden = bd.op("gpsimd", "tensor_add", de[:, 0:G], de[:, 0:G],
                             bb[0][:])
                bd.wait("scalar", "msP", cden)
                bd.op("scalar", "activation", bb[1][:], de[:, 0:G], AF.Ln)
                crcp = bd.op("scalar", "activation", bb[1][:], bb[1][:], AF.Exp,
                             scale=-1.0)
                crcp_prev = crcp
                bd.wait("vector", "msA", crcp)
                bd.op("vector", "tensor_tensor", no[:, 0:G], no[:, 0:G], bb[1][:],
                      ALU.mult)
                bd.wait("vector", "vqs%d" % ssl, csr)
                cvo2 = bd.op(
                    "vector", "tensor_tensor",
                    rwkv[:, mo * TCORE + t0 - HALO: mo * TCORE + t0 - HALO + nt],
                    no[:, 0:G], srin[ssl][:], ALU.mult)
                srin_free[ssl] = cvo2
                numo_free[pp] = cden
            if gi >= 1:
                crw_g[gi] = (bd.C["msV"], bd.C["msP"])
                if gi >= 2:
                    emit_wo(gi - 1)
        emit_wo(len(GIN) - 1)
    stp.close()
    bd.barrier()
def emit_out(bd, cfg):
    TCORE, G, NF = cfg["TCORE"], cfg["G"], cfg["NF"]
    with contextlib.ExitStack() as st:
        xin = bd.sb(st, "oxin", [128, NF * G], F32)
        accin = bd.sb(st, "oacc", [128, NF * G], F32)
        outt = bd.sb(st, "outt", [128, NF * G], F32)
        bd.drain("sync", ["f2qacc0", "f2qacc1", "bqxa0", "bqxa1"])
        free_V = 0
        out_done = 0
        for (t0, nt) in cfg["G0"]:
            bd.wait("sync", "msV", free_V)
            cx = bd.dma("sync", "oqx", row3(xin[:]).rearrange(
                "p o (f t) -> p (o f) t", t=nt), fm(bd.d["xattn_d"], 0, NF, t0, nt))
            cac = bd.dma("sync", "oqa", row3(accin[:]).rearrange(
                "p o (f t) -> p (o f) t", t=nt), fm(bd.d["acc_d"], 0, NF, t0, nt))
            bd.wait("vector", "oqx", cx)
            bd.wait("vector", "oqa", cac)
            bd.wait("vector", "oqout", out_done)
            for f in range(NF):
                bd.op("vector", "scalar_tensor_tensor", outt[:, f * nt:(f + 1) * nt],
                      accin[:, f * nt:(f + 1) * nt], 0.5, xin[:, f * nt:(f + 1) * nt],
                      ALU.mult, ALU.add, inc="msV" if f == NF - 1 else None)
            free_V = bd.C["msV"]
            bd.wait("scalar", "msV", free_V)
            out_done = bd.dma("scalar", "oqout", fm(bd.d["y_out"], 0, NF, t0, nt),
                              row3(outt[:]).rearrange("p o (f t) -> p (o f) t", t=nt))
        bd.wait("scalar", "oqout", out_done)
        bd.wait("gpsimd", "oqout", out_done)


def build_program(cfg):
    cfg = derive(cfg)
    DM, H, E, TCORE, G = cfg["DM"], cfg["H"], cfg["E"], cfg["TCORE"], cfg["G"]
    NF, TCIN = cfg["NF"], cfg["TCIN"]

    nc = bass.Bass("TRN2", target_bir_lowering=False)
    d = {}
    d["x_in"] = nc.dram_tensor("x", [NF, 128, TCIN], F32, kind="ExternalInput")
    d["wg"] = nc.dram_tensor("wg", [E, DM, H], F32, kind="ExternalInput")
    d["wu"] = nc.dram_tensor("wu", [E, DM, H], F32, kind="ExternalInput")
    d["wd"] = nc.dram_tensor("wd", [E, H, DM], F32, kind="ExternalInput")
    d["sg"] = nc.dram_tensor("sg", [DM, 2 * H], F32, kind="ExternalInput")
    d["su"] = nc.dram_tensor("su", [DM, 2 * H], F32, kind="ExternalInput")
    d["sd"] = nc.dram_tensor("sd", [2 * H, DM], F32, kind="ExternalInput")
    for n in ("wkT", "wvT", "wrT", "woT"):
        d[n] = nc.dram_tensor(n, [DM, DM], F32, kind="ExternalInput")
    d["routT"] = nc.dram_tensor("routT", [128, NF * E], F32, kind="ExternalInput")
    for n in ("td", "tf", "tmk", "tmv", "tmr"):
        d[n] = nc.dram_tensor(n, [128, NF], F32, kind="ExternalInput")
    d["hv"] = nc.dram_tensor("hv", [128, 1], F32, kind="ExternalInput")
    d["ident"] = nc.dram_tensor("ident", [128, 128], F32, kind="ExternalInput")
    d["y_out"] = nc.dram_tensor("y", [NF, 128, TCORE], F32, kind="ExternalOutput")
    dbg = "ExternalOutput" if DEBUG_INTERNALS else None
    for n in ("acc_d", "xmid_d", "ek_d", "ekv_d", "sr_d"):
        d[n] = nc.dram_tensor(n, [NF, 128, TCIN], F32, kind=dbg) if dbg else \
            nc.dram_tensor(n, [NF, 128, TCIN], F32)
    d["xattn_d"] = nc.dram_tensor("xattn_d", [NF, 128, TCORE], F32, kind=dbg) if dbg \
        else nc.dram_tensor("xattn_d", [NF, 128, TCORE], F32)
    d["wvc"] = nc.dram_tensor("wvc", [DM, DM], BF16)
    d["wkc"] = nc.dram_tensor("wkc", [DM, DM], BF16)
    d["wrc"] = nc.dram_tensor("wrc", [DM, DM], BF16)
    d["woc"] = nc.dram_tensor("woc", [DM, DM], BF16)
    d["gates_d"] = nc.dram_tensor("gates_d", [len(cfg["GIN"]) * E, G], F32)

    with contextlib.ExitStack() as st:
        bd = B(nc, st, cfg)
        bd.d = d
        t = bd.t
        for n, sh in [("ones_col", [128, 1]), ("ones_row", [1, 128]),
                      ("ones_g", [128, G]), ("ident_sb", [128, 128]),
                      ("ew", [128, NF]), ("eu", [128, NF]), ("ewrep", [128, G]),
                      ("tmk_sb", [128, NF]), ("tmv_sb", [128, NF]), ("tmr_sb", [128, NF]),
                      ("tdsb", [128, NF]), ("tfsb", [128, NF]), ("hv_sb", [128, 1]),
                      ("routT_sb", [128, NF * E]),
                      ("numcar", [128, NF]), ("dencar", [128, NF])]:
            t[n] = bd.sb(st, n, sh, F32)

        for n, dst in [("td", "tdsb"), ("tf", "tfsb"), ("tmk", "tmk_sb"),
                       ("tmv", "tmv_sb"), ("tmr", "tmr_sb"), ("ident", "ident_sb"),
                       ("hv", "hv_sb"), ("routT", "routT_sb")]:
            ci = bd.dma("sync", "qinit", t[dst][:], d[n].ap())
        bd.op("vector", "memset", t["ones_col"][:], 1.0)
        bd.op("vector", "memset", t["ones_row"][:], 1.0)
        bd.op("vector", "memset", t["ones_g"][:], 1.0)
        bd.op("vector", "memset", t["numcar"][:], 0.0)
        cv = bd.op("vector", "memset", t["dencar"][:], 0.0, inc="msV")
        bd.wait("scalar", "qinit", ci)
        bd.op("scalar", "activation", t["ew"][:], t["tdsb"][:], AF.Exp)
        bd.op("scalar", "activation", t["ew"][:], t["ew"][:], AF.Exp, scale=-1.0)
        ca = bd.op("scalar", "activation", t["eu"][:], t["tfsb"][:], AF.Exp, inc="msA")
        for eng in B.ENGS:
            bd.wait(eng, "qinit", ci)
            bd.wait(eng, "msV", cv)
            bd.wait(eng, "msA", ca)

        emit_ffn(bd, cfg, d["x_in"], "f1", cfg["GIN"], TCIN, cache=True)
        emit_attn(bd, cfg)
        emit_ffn(bd, cfg, d["xattn_d"], "f2", cfg["G0"], TCORE,
                 src_ready_q=["bqxa0", "bqxa1"], final_merge=True)

        block = st.enter_context(nc.Block())

        @block.tensor
        def _(e):
            for fn in bd.q["tensor"]:
                fn(e)

        @block.vector
        def _(e):
            for fn in bd.q["vector"]:
                fn(e)

        @block.scalar
        def _(e):
            for fn in bd.q["scalar"]:
                fn(e)

        @block.gpsimd
        def _(e):
            for fn in bd.q["gpsimd"]:
                fn(e)

        @block.sync
        def _(e):
            for fn in bd.q["sync"]:
                fn(e)

    return nc


def make_in_maps(cfg, inputs):
    cfg = derive(cfg)
    DM, H, E, TCORE = cfg["DM"], cfg["H"], cfg["E"], cfg["TCORE"]
    NF, TCIN, HALO = cfg["NF"], cfg["TCIN"], cfg["HALO"]
    CORES, CPB = 8, 4

    x = np.ascontiguousarray(np.asarray(inputs["L0_attn_x"], dtype=np.float32))
    Bb, T, C = x.shape
    x_fm = np.ascontiguousarray(np.transpose(x, (0, 2, 1)))

    def chan_tile(v):
        return np.ascontiguousarray(
            np.asarray(v, np.float32).reshape(-1)[0:DM].reshape(NF, 128).T)

    rt = np.asarray(inputs["L0_ffn_router_w"], np.float32)
    routT = np.ascontiguousarray(
        rt.T.reshape(NF, 128, E).transpose(1, 0, 2).reshape(128, NF * E))

    base = dict(
        wg=np.ascontiguousarray(np.asarray(inputs["L0_ffn_wg"], np.float32)),
        wu=np.ascontiguousarray(np.asarray(inputs["L0_ffn_wu"], np.float32)),
        wd=np.ascontiguousarray(np.asarray(inputs["L0_ffn_wd"], np.float32)),
        sg=np.ascontiguousarray(np.asarray(inputs["L0_ffn_sg"], np.float32)),
        su=np.ascontiguousarray(np.asarray(inputs["L0_ffn_su"], np.float32)),
        sd=np.ascontiguousarray(np.asarray(inputs["L0_ffn_sd"], np.float32)),
        wkT=np.ascontiguousarray(np.asarray(inputs["L0_attn_Wk"], np.float32).T),
        wvT=np.ascontiguousarray(np.asarray(inputs["L0_attn_Wv"], np.float32).T),
        wrT=np.ascontiguousarray(np.asarray(inputs["L0_attn_Wr"], np.float32).T),
        woT=np.ascontiguousarray(np.asarray(inputs["L0_attn_Wo"], np.float32).T),
        routT=routT,
        td=chan_tile(inputs["L0_attn_time_decay"]),
        tf=chan_tile(inputs["L0_attn_time_first"]),
        tmk=chan_tile(inputs["L0_attn_time_mix_k"]),
        tmv=chan_tile(inputs["L0_attn_time_mix_v"]),
        tmr=chan_tile(inputs["L0_attn_time_mix_r"]),
        ident=np.eye(128, dtype=np.float32),
    )
    maps = []
    for c in range(CORES):
        b, i = c // CPB, c % CPB
        m = dict(base)
        xs = np.zeros((C, TCIN), np.float32)
        lo = i * TCORE - HALO
        src_lo = max(lo, 0)
        xs[:, src_lo - lo:] = x_fm[b][:, src_lo: (i + 1) * TCORE]
        m["x"] = np.ascontiguousarray(xs.reshape(NF, 128, TCIN))
        m["hv"] = np.full((128, 1), 0.0 if i == 0 else 1.0, np.float32)
        maps.append(m)
    return maps


def assemble_out(cfg, results, B_, T, C):
    cfg = derive(cfg)
    TCORE = cfg["TCORE"]
    CPB = 4
    out_fm = np.empty((B_, C, T), np.float32)
    for c in range(8):
        b, i = c // CPB, c % CPB
        out_fm[b][:, i * TCORE:(i + 1) * TCORE] = \
            np.asarray(results[c]["y"]).reshape(C, TCORE)
    return np.ascontiguousarray(np.transpose(out_fm, (0, 2, 1)))


def kernel(**inputs):
    cfg = FULL_CFG
    x = np.asarray(inputs["L0_attn_x"])
    B_, T, C = x.shape
    nc = build_program(cfg)
    maps = make_in_maps(cfg, inputs)
    res = run_bass_kernel_spmd(nc, maps, core_ids=list(range(8)))
    return assemble_out(cfg, res.results, B_, T, C)


# version 20 (final)
# speedup vs baseline: 1.1336x; 1.1336x over previous
import contextlib
import numpy as np

import concourse.bass as bass
import concourse.mybir as mybir
from concourse.bass_utils import run_bass_kernel_spmd

F32 = mybir.dt.float32
BF16 = mybir.dt.bfloat16
ALU = mybir.AluOpType
AF = mybir.ActivationFunctionType
AX = mybir.AxisListType

FULL_CFG = dict(DM=2048, H=1408, E=8, TCORE=2048, G=512, HALO=128)
DEBUG_INTERNALS = False
LN_EPS = 1e-5
GATE_EPS = 1e-20


def derive(cfg):
    d = dict(cfg)
    d["NF"] = cfg["DM"] // 128
    d["NHM"] = cfg["H"] // 128
    d["TCIN"] = cfg["TCORE"] + cfg["HALO"]
    G = cfg["G"]
    d["GIN"] = [(0, cfg["HALO"])] + [(cfg["HALO"] + G * k, G)
                                     for k in range(cfg["TCORE"] // G)]
    d["GOWN"] = [(cfg["HALO"] + G * k, G) for k in range(cfg["TCORE"] // G)]
    d["G0"] = [(G * k, G) for k in range(cfg["TCORE"] // G)]
    return d


class B:
    ENGS = ("tensor", "vector", "scalar", "gpsimd", "sync")
    AUTO = {"vector": "msV", "scalar": "msA", "gpsimd": "msP"}

    def __init__(self, nc, st, cfg):
        self.nc, self.st, self.cfg = nc, st, cfg
        self.q = {e: [] for e in self.ENGS}
        self.S, self.C = {}, {}
        self.t = {}
        self.d = {}
        self.qeng = {}
        self.last_auto = {}
        self.barn = 0

    def sem(self, name):
        if name not in self.S:
            self.S[name] = self.st.enter_context(self.nc.semaphore(name))
            self.C[name] = 0
        return self.S[name]

    def op(self, eng, meth, *args, inc=None, n=1, **kw):
        if eng in self.AUTO:
            if eng in self.last_auto:
                self.wait(eng, *self.last_auto[eng])
            name = inc if inc is not None else self.AUTO[eng]
            s = self.sem(name)
            self.C[name] += n
            cnt = self.C[name]
            self.last_auto[eng] = (name, cnt)
            self.q[eng].append(
                lambda e, m=meth, a=args, k=kw, s=s, n=n: getattr(e, m)(*a, **k).then_inc(s, n))
            return cnt
        cnt = None
        if inc is not None:
            s = self.sem(inc)
            self.C[inc] += n
            cnt = self.C[inc]
            self.q[eng].append(
                lambda e, m=meth, a=args, k=kw, s=s, n=n: getattr(e, m)(*a, **k).then_inc(s, n))
        else:
            self.q[eng].append(lambda e, m=meth, a=args, k=kw: getattr(e, m)(*a, **k))
        return cnt

    def wait(self, eng, name, val):
        if val is None or val <= 0:
            return
        s = self.sem(name)
        self.q[eng].append(lambda e, s=s, v=val: e.wait_ge(s, v))

    def dma(self, eng, qname, out, in_, **kw):
        if eng in self.AUTO and eng in self.last_auto:
            self.wait(eng, *self.last_auto[eng])
        s = self.sem(qname)
        self.C[qname] += 16
        cnt = self.C[qname]
        self.q[eng].append(
            lambda e, o=out, i=in_, s=s, k=kw: e.dma_start(out=o, in_=i, **k).then_inc(s, 16))
        self.qeng[qname] = eng
        return cnt

    def inc(self, eng, name, n=1):
        s = self.sem(name)
        self.C[name] += n
        cnt = self.C[name]
        self.q[eng].append(lambda e, s=s, n=n: e.sem_inc(s, n))
        return cnt

    def cur(self, name):
        return self.C.get(name, 0)

    def drain(self, eng, qnames):
        for qn in qnames:
            self.wait(eng, qn, self.cur(qn))

    def barrier(self):
        self.barn += 1
        for qn, eng in self.qeng.items():
            self.wait(eng, qn, self.cur(qn))
        for eng in self.ENGS:
            if eng in self.AUTO and eng in self.last_auto:
                self.wait(eng, *self.last_auto[eng])
            if eng == "tensor":
                self.q[eng].append(lambda e: e.drain())
            self.inc(eng, "bar_" + eng)
        for eng in self.ENGS:
            for o in self.ENGS:
                if o != eng:
                    self.wait(eng, "bar_" + o, self.barn)

    def sb(self, st, name, shape, dt):
        return st.enter_context(self.nc.sbuf_tensor(name, shape, dt))

    def ps(self, st, name, shape, dt=F32):
        return st.enter_context(self.nc.psum_tensor(name, shape, dt))


def fm(dram, f0, nf, t0, nt):
    return dram.ap().rearrange("f p t -> p f t")[:, f0:f0 + nf, t0:t0 + nt]


def row3(sb_ap):
    return sb_ap.rearrange("p (o t) -> p o t", o=1)


def emit_ln_group(bd, cfg, get_x, nt, sc, prev):
    NF, DM = cfg["NF"], cfg["DM"]
    (p_sum, p_sumsq, p_ab, sqt, rows) = sc
    mu, var, alf, bet = [r[0:1, 0:nt] for r in rows]
    p_s = p_sum[0:1, 0:nt]
    p_sq = p_sumsq[0:1, 0:nt]
    bd.wait("scalar", "msV", prev.get("sq_free", 0))
    bd.wait("tensor", "msV", prev.get("stats_free", 0))
    sq_mm = [0, 0]
    for f in range(NF):
        xap = get_x(f)
        bd.wait("scalar", "msT", sq_mm[f % 2])
        ca = bd.op("scalar", "activation", sqt[f % 2][:, 0:nt], xap, AF.Square,
                   inc="msA")
        bd.op("tensor", "matmul", p_s, bd.t["ones_col"][:], xap,
              start=(f == 0), stop=(f == NF - 1))
        bd.wait("tensor", "msA", ca)
        sq_mm[f % 2] = bd.op("tensor", "matmul", p_sq, bd.t["ones_col"][:],
                             sqt[f % 2][:, 0:nt], start=(f == 0), stop=(f == NF - 1),
                             inc="msT")
    cstat = bd.C["msT"]
    bd.wait("vector", "msT", cstat)
    bd.op("vector", "tensor_scalar_mul", mu, p_s, 1.0 / DM)
    bd.op("vector", "tensor_scalar", var, p_sq, 1.0 / DM, LN_EPS,
          ALU.mult, ALU.add)
    bd.op("vector", "scalar_tensor_tensor", alf, mu, -1.0, mu,
          ALU.mult, ALU.mult)
    cvar = bd.op("vector", "tensor_tensor", var, var, alf, ALU.add, inc="msV")
    bd.wait("scalar", "msV", cvar)
    bd.op("scalar", "activation", var, var, AF.Ln)
    ca = bd.op("scalar", "activation", alf, var, AF.Exp, scale=-0.5, inc="msA")
    bd.wait("vector", "msA", ca)
    cbet = bd.op("vector", "scalar_tensor_tensor", bet, mu, -1.0, alf,
                 ALU.mult, ALU.mult, inc="msV")
    bd.wait("tensor", "msV", cbet)
    bd.wait("tensor", "msV", prev.get("pab_free", 0))
    bd.op("tensor", "matmul", p_ab[0][:, 0:nt], bd.t["ones_row"][:], alf,
          start=True, stop=True)
    cbc = bd.op("tensor", "matmul", p_ab[1][:, 0:nt], bd.t["ones_row"][:], bet,
                start=True, stop=True, inc="msT")
    bd.wait("vector", "msT", cbc)
    return cbc, cvar


def emit_ffn(bd, cfg, src, tag, groups, TL, src_ready_q=(), final_merge=False):
    DM, H, E, G = cfg["DM"], cfg["H"], cfg["E"], cfg["G"]
    NF, NHM = cfg["NF"], cfg["NHM"]
    NGR = len(groups)
    stx = contextlib.ExitStack()
    xf = bd.sb(stx, tag + "xf", [128, NF * TL], BF16)

    with contextlib.ExitStack() as st:
        xin = [bd.sb(st, tag + "xin%d" % i, [128, NF * G], F32) for i in range(2)]
        sqt = [bd.sb(st, tag + "sq%d" % i, [128, G], F32) for i in range(2)]
        tfp = [bd.sb(st, tag + "tfp%d" % i, [128, G], F32) for i in range(2)]
        rows = [bd.sb(st, tag + "row%d" % i, [1, G], F32) for i in range(4)]
        r8sb = bd.sb(st, tag + "r8sb", [8, G], F32)
        gtm = bd.sb(st, tag + "gtm", [128, 8], F32)
        gx = [bd.sb(st, tag + "gx%d" % i, [128, 8], F32) for i in range(3)]
        gcol = [bd.sb(st, tag + "gc%d" % i, [128, 1], F32) for i in range(4)]
        gem = bd.sb(st, tag + "gem", [8, G], F32)
        p_sum = bd.ps(st, tag + "p_sum", [1, G])
        p_sumsq = bd.ps(st, tag + "p_sumsq", [1, G])
        p_ab = [bd.ps(st, tag + "p_ab%d" % i, [128, G]) for i in range(2)]
        p_r8 = bd.ps(st, tag + "p_r8", [8, G])
        p_tp = bd.ps(st, tag + "p_tp", [128, 8])
        p_gem = bd.ps(st, tag + "p_gem", [8, 128])

        bd.drain("sync", src_ready_q)
        xin_done = [0, 0]
        prev = {}
        tfp_T = [0, 0]
        tfp_P = [0, 0]
        qg_done = 0
        ptp_free = [0]
        for gi, (t0, nt) in enumerate(groups):
            slot = gi % 2
            nchunks = nt // 128
            bd.wait("sync", "msT", xin_done[slot])
            cx = bd.dma("sync", tag + "qx%d" % slot,
                        row3(xin[slot][:, 0:NF * nt]).rearrange(
                            "p o (f t) -> p (o f) t", t=nt),
                        fm(src, 0, NF, t0, nt))
            bd.wait("scalar", tag + "qx%d" % slot, cx)
            bd.wait("tensor", tag + "qx%d" % slot, cx)
            cbc, cvar = emit_ln_group(
                bd, cfg, lambda f, s=slot, w=nt: xin[s][:, f * w:(f + 1) * w], nt,
                (p_sum, p_sumsq, p_ab, sqt, rows), prev)
            for f in range(NF):
                fs = f % 2
                xap = xin[slot][:, f * nt:(f + 1) * nt]
                bd.wait("vector", "msT", tfp_T[fs])
                bd.wait("vector", "msP", tfp_P[fs])
                bd.op("vector", "tensor_tensor", tfp[fs][:, 0:nt], xap,
                      p_ab[0][:, 0:nt], ALU.mult)
                cv = bd.op("vector", "tensor_tensor", tfp[fs][:, 0:nt],
                           tfp[fs][:, 0:nt], p_ab[1][:, 0:nt], ALU.add, inc="msV")
                bd.wait("gpsimd", "msV", cv)
                tfp_P[fs] = bd.op("gpsimd", "tensor_copy",
                                  xf[:, f * TL + t0: f * TL + t0 + nt],
                                  tfp[fs][:, 0:nt], inc="msP")
                bd.wait("tensor", "msV", cv)
                tfp_T[fs] = bd.op("tensor", "matmul", p_r8[0:8, 0:nt],
                                  bd.t["routT_sb"][:, f * E:(f + 1) * E],
                                  tfp[fs][:, 0:nt],
                                  start=(f == 0), stop=(f == NF - 1), inc="msT")
            xin_done[slot] = crt = bd.C["msT"]
            apply_V = bd.C["msV"]
            prev = dict(stats_free=cvar, pab_free=apply_V, sq_free=apply_V)
            bd.wait("vector", "msT", crt)
            cr8 = bd.op("vector", "tensor_copy", r8sb[0:8, 0:nt], p_r8[0:8, 0:nt],
                        inc="msV")
            gem_V = 0
            for cc in range(nchunks):
                bd.wait("tensor", "msV", cr8)
                bd.wait("tensor", "msV", ptp_free[0])
                ct = bd.op("tensor", "transpose", p_tp[:],
                           r8sb[0:8, cc * 128:(cc + 1) * 128],
                           bd.t["ident_sb"][0:8, 0:8], inc="msT")
                bd.wait("vector", "msT", ct)
                ptp_free[0] = bd.op("vector", "tensor_copy", gtm[:], p_tp[:])
                bd.op("vector", "reduce_max", gcol[0][:], gtm[:], axis=AX.X)
                cneg = bd.op("vector", "tensor_scalar_mul", gcol[1][:], gcol[0][:],
                             -1.0, inc="msV")
                bd.wait("scalar", "msV", cneg)
                ca = bd.op("scalar", "activation", gx[0][:], gtm[:], AF.Exp,
                           bias=gcol[1][:, 0:1], accum_out=gcol[2][:, 0:1], inc="msA")
                bd.wait("vector", "msA", ca)
                bd.op("vector", "reciprocal", gcol[3][:], gcol[2][:])
                bd.op("vector", "tensor_scalar_mul", gx[0][:], gx[0][:], gcol[3][:, 0:1])
                bd.op("vector", "reduce_max", gcol[0][:], gx[0][:], axis=AX.X)
                bd.op("vector", "tensor_single_scalar", gx[1][:], gx[0][:],
                      gcol[0][:, 0:1], ALU.is_ge)
                bd.op("vector", "tensor_scalar", gx[1][:], gx[1][:], -1.0, 1.0,
                      ALU.mult, ALU.add)
                bd.op("vector", "tensor_tensor", gx[1][:], gx[0][:], gx[1][:], ALU.mult)
                bd.op("vector", "reduce_max", gcol[1][:], gx[1][:], axis=AX.X)
                bd.op("vector", "tensor_tensor", gcol[2][:], gcol[0][:], gcol[1][:],
                      ALU.add)
                bd.op("vector", "tensor_scalar_add", gcol[2][:], gcol[2][:], GATE_EPS)
                bd.op("vector", "reciprocal", gcol[3][:], gcol[2][:])
                bd.op("vector", "tensor_single_scalar", gx[2][:], gx[0][:],
                      gcol[1][:, 0:1], ALU.is_ge)
                bd.op("vector", "tensor_tensor", gx[2][:], gx[0][:], gx[2][:], ALU.mult)
                cfin = bd.op("vector", "tensor_scalar_mul", gx[2][:], gx[2][:],
                             gcol[3][:, 0:1], inc="msV")
                bd.wait("tensor", "msV", cfin)
                if gem_V:
                    bd.wait("tensor", "msV", gem_V)
                ct2 = bd.op("tensor", "transpose", p_gem[:], gx[2][:],
                            bd.t["ident_sb"][:], inc="msT")
                bd.wait("vector", "msT", ct2)
                if qg_done:
                    bd.wait("vector", tag + "qg", qg_done)
                gem_V = bd.op("vector", "tensor_copy",
                              gem[0:8, cc * 128:(cc + 1) * 128], p_gem[:], inc="msV")
            bd.wait("sync", "msV", gem_V)
            qg_done = bd.dma("sync", tag + "qg",
                             bd.d["gates_d"].ap()[gi * E:(gi + 1) * E, 0:nt],
                             gem[0:8, 0:nt])
    bd.barrier()

    with contextlib.ExitStack() as st:
        wslab = [bd.sb(st, tag + "ws%d" % i, [128, NF * 128], F32) for i in range(2)]
        wbf_g = [bd.sb(st, tag + "wbg%d" % i, [128, NF * 128], BF16) for i in range(2)]
        wbf_u = [bd.sb(st, tag + "wbu%d" % i, [128, NF * 128], BF16) for i in range(2)]
        wdbf = [bd.sb(st, tag + "wdb%d" % i, [128, NHM * 128], BF16) for i in range(2)]
        hbuf = bd.sb(st, tag + "h", [128, NHM * TL], BF16)
        stage = [bd.sb(st, tag + "stage%d" % i, [128, TL], F32) for i in range(2)]
        gbc = bd.sb(st, tag + "gbc", [128, TL], F32)
        grow = bd.sb(st, tag + "grow", [1, G], F32)
        if final_merge:
            mgacc = [bd.sb(st, tag + "mga0", [128, TL], F32)] * 2
            mgxat = [bd.sb(st, tag + "mgx0", [128, TL], F32)] * 2
        stmp = [bd.sb(st, tag + "st%d" % i, [128, G], F32) for i in range(2)]
        ttmp = [bd.sb(st, tag + "tt%d" % i, [128, G], F32) for i in range(2)]
        p_g = [bd.ps(st, tag + "p_g%d" % i, [128, G]) for i in range(3)]
        p_u = [bd.ps(st, tag + "p_u%d" % i, [128, G]) for i in range(3)]
        p_d = [bd.ps(st, tag + "p_d%d" % i, [128, G]) for i in range(2)]
        p_bc = p_d[0]

        ws_free = [None, None]
        wbg_free = [0, 0]
        wbu_free = [0, 0]
        wd_free = 0
        wdb_free = [0, 0]
        pg_free = [0, 0, 0]
        pu_free = [0, 0, 0]
        pd_free = [0, 0]
        stmp_free = [0, 0]
        ttmp_free = [0, 0]
        stage_free = [0, 0]
        mg_free = [0, 0]
        out_free = [0, 0]
        gbc_free = 0
        grow_free = 0
        pbc_free = 0
        bd.wait("sync", tag + "qg", bd.cur(tag + "qg"))
        for p in range(E + 2):
            routed = p < E
            if routed:
                wg_ap = bd.d["wg"].ap()[p].rearrange("(kc q) h -> q kc h", q=128)
                wu_ap = bd.d["wu"].ap()[p].rearrange("(kc q) h -> q kc h", q=128)
                wd_ap = bd.d["wd"].ap()[p].rearrange("(hc q) d -> q hc d", q=128)
                h0 = 0
            else:
                wg_ap = bd.d["sg"].ap().rearrange("(kc q) h -> q kc h", q=128)
                wu_ap = bd.d["su"].ap().rearrange("(kc q) h -> q kc h", q=128)
                wd_ap = bd.d["sd"].ap().rearrange("(hc q) d -> q hc d", q=128)
                h0 = (p - E) * H
            if routed:
                cgbc = 0
                for gi, (t0, nt) in enumerate(groups):
                    bd.wait("sync", "msT", grow_free)
                    cgr = bd.dma("sync", tag + "qgr", grow[0:1, 0:nt],
                                 bd.d["gates_d"].ap()[gi * E + p: gi * E + p + 1, 0:nt])
                    bd.wait("tensor", tag + "qgr", cgr)
                    bd.wait("tensor", "msP", gbc_free)
                    bd.wait("tensor", "msV", max(pbc_free, pd_free[0]))
                    ct = bd.op("tensor", "matmul", p_bc[:, 0:nt], bd.t["ones_row"][:],
                               grow[0:1, 0:nt], start=True, stop=True, inc="msT")
                    grow_free = ct
                    bd.wait("vector", "msT", ct)
                    cgbc = pbc_free = bd.op(
                        "vector", "tensor_copy", gbc[:, t0:t0 + nt], p_bc[:, 0:nt],
                        inc="msV")
                pd_free[0] = max(pd_free[0], pbc_free)
            def load_gu(mh):
                sl = mh % 2
                if ws_free[sl]:
                    bd.wait("sync", *ws_free[sl])
                cg = bd.dma("sync", tag + "qw%d" % sl,
                            wslab[sl][:].rearrange("p (kc h) -> p kc h", h=128),
                            wg_ap[:, :, h0 + mh * 128: h0 + (mh + 1) * 128])
                bd.wait("vector", tag + "qw%d" % sl, cg)
                bd.wait("vector", "msT", wbg_free[sl])
                cvg = bd.op("vector", "tensor_copy", wbf_g[sl][:], wslab[sl][:])
                bd.wait("sync", "msV", cvg)
                cu = bd.dma("sync", tag + "qw%d" % sl,
                            wslab[sl][:].rearrange("p (kc h) -> p kc h", h=128),
                            wu_ap[:, :, h0 + mh * 128: h0 + (mh + 1) * 128])
                bd.wait("scalar", tag + "qw%d" % sl, cu)
                bd.wait("scalar", "msT", wbu_free[sl])
                cvu = bd.op("scalar", "copy", wbf_u[sl][:], wslab[sl][:])
                ws_free[sl] = ("msA", cvu)
                return cvg, cvu

            pend = load_gu(0)
            for mh in range(NHM):
                sl = mh % 2
                cvg, cvu = pend
                if mh + 1 < NHM:
                    pend = load_gu(mh + 1)
                for gi, (t0, nt) in enumerate(groups):
                    gp = gi % 3
                    bd.wait("tensor", "msV", cvg)
                    bd.wait("tensor", "msA", pg_free[gp])
                    for kc in range(NF):
                        cmg = bd.op("tensor", "matmul", p_g[gp][:, 0:nt],
                                    wbf_g[sl][:, kc * 128:(kc + 1) * 128],
                                    xf[:, kc * TL + t0: kc * TL + t0 + nt],
                                    start=(kc == 0), stop=(kc == NF - 1),
                                    inc="msT" if kc == NF - 1 else None)
                    bd.wait("tensor", "msA", cvu)
                    bd.wait("tensor", "msV", pu_free[gp])
                    for kc in range(NF):
                        cmu = bd.op("tensor", "matmul", p_u[gp][:, 0:nt],
                                    wbf_u[sl][:, kc * 128:(kc + 1) * 128],
                                    xf[:, kc * TL + t0: kc * TL + t0 + nt],
                                    start=(kc == 0), stop=(kc == NF - 1),
                                    inc="msT" if kc == NF - 1 else None)
                    if gi == NGR - 1:
                        wbg_free[sl] = cmg
                        wbu_free[sl] = cmu
                    ssl = (mh * NGR + gi) % 2
                    bd.wait("scalar", "msT", cmg)
                    bd.wait("scalar", "msV", stmp_free[ssl])
                    ca = bd.op("scalar", "activation", stmp[ssl][:, 0:nt],
                               p_g[gp][:, 0:nt], AF.Silu, inc="msA")
                    pg_free[gp] = ca
                    bd.wait("vector", "msA", ca)
                    bd.wait("vector", "msT", cmu)
                    bd.wait("vector", "msP", ttmp_free[ssl])
                    cv = bd.op("vector", "tensor_tensor", ttmp[ssl][:, 0:nt],
                               stmp[ssl][:, 0:nt], p_u[gp][:, 0:nt], ALU.mult, inc="msV")
                    pu_free[gp] = cv
                    stmp_free[ssl] = cv
                    bd.wait("gpsimd", "msV", cv)
                    hsl = hbuf[:, mh * TL + t0: mh * TL + t0 + nt]
                    if routed:
                        bd.wait("gpsimd", "msV", cgbc)
                        cp = bd.op("gpsimd", "tensor_tensor", hsl, ttmp[ssl][:, 0:nt],
                                   gbc[:, t0:t0 + nt], ALU.mult, inc="msP")
                    else:
                        cp = bd.op("gpsimd", "tensor_copy", hsl, ttmp[ssl][:, 0:nt],
                                   inc="msP")
                    ttmp_free[ssl] = cp
            hdone = bd.C["msP"]
            if routed:
                gbc_free = hdone
            def load_wd(m):
                sl = m % 2
                stg = wslab[sl][:, 0:NHM * 128]
                fr = wd_stage_free[sl]
                if fr:
                    bd.wait("sync", *fr)
                cd = bd.dma("sync", tag + "qd%d" % sl,
                            stg.rearrange("p (hc d) -> p hc d", d=128),
                            wd_ap[:, h0 // 128: h0 // 128 + NHM, m * 128:(m + 1) * 128])
                bd.wait("vector", tag + "qd%d" % sl, cd)
                bd.wait("vector", "msT", wdb_free[sl])
                cvd = bd.op("vector", "tensor_copy", wdbf[sl][:], stg)
                wd_stage_free[sl] = ("msV", cvd)
                if sl == 1:
                    ws_free[0] = ws_free[0]
                return cvd

            wd_stage_free = [ws_free[0], ws_free[1]]
            pend_d = load_wd(0)
            for m in range(NF):
                sl = m % 2
                cvd = pend_d
                if m + 1 < NF:
                    pend_d = load_wd(m + 1)
                last = final_merge and p == E + 1
                if last:
                    bd.wait("sync", "msV", mg_free[0])
                    bd.wait("sync", tag + "qacc%d" % (m % 2), stage_free[m % 2])
                    cacc = bd.dma("sync", tag + "qmg0",
                                  row3(mgacc[0][:]), fm(bd.d["acc_d"], m, 1, 0, TL))
                    cxat = bd.dma("sync", tag + "qmx0",
                                  row3(mgxat[0][:]), fm(src, m, 1, 0, TL))
                for gi, (t0, nt) in enumerate(groups):
                    dp = gi % 2
                    bd.wait("tensor", "msV", cvd)
                    bd.wait("tensor", "msP", hdone)
                    bd.wait("tensor", "msV", pd_free[dp])
                    if routed and dp == 0:
                        bd.wait("tensor", "msV", pbc_free)
                    for hc in range(NHM):
                        cmd = bd.op("tensor", "matmul", p_d[dp][:, 0:nt],
                                    wdbf[sl][:, hc * 128:(hc + 1) * 128],
                                    hbuf[:, hc * TL + t0: hc * TL + t0 + nt],
                                    start=(hc == 0), stop=(hc == NHM - 1),
                                    inc="msT" if hc == NHM - 1 else None)
                    if gi == NGR - 1:
                        wdb_free[sl] = cmd
                    bd.wait("vector", "msT", cmd)
                    if last:
                        bd.wait("vector", tag + "qmg0", cacc)
                        bd.wait("vector", tag + "qmx0", cxat)
                        bd.wait("vector", tag + "qout%d" % (m % 2), out_free[m % 2])
                        bd.op("vector", "tensor_tensor", stage[m % 2][:, t0:t0 + nt],
                              p_d[dp][:, 0:nt], mgacc[0][:, t0:t0 + nt], ALU.add)
                        cv = bd.op("vector", "scalar_tensor_tensor",
                                   stage[m % 2][:, t0:t0 + nt],
                                   stage[m % 2][:, t0:t0 + nt], 0.5,
                                   mgxat[0][:, t0:t0 + nt], ALU.mult, ALU.add,
                                   inc="msV")
                        pd_free[dp] = cv
                    else:
                        if gi == 0:
                            bd.wait("vector", tag + "qacc%d" % (m % 2),
                                    stage_free[m % 2])
                        cv = bd.op("vector", "tensor_copy",
                                   stage[m % 2][:, t0:t0 + nt],
                                   p_d[dp][:, 0:nt], inc="msV")
                        pd_free[dp] = cv
                ws_free[sl] = wd_stage_free[sl]
                if last:
                    mg_free[0] = bd.C["msV"]
                    bd.wait("scalar", "msV", bd.C["msV"])
                    out_free[m % 2] = bd.dma("scalar", tag + "qout%d" % (m % 2),
                                             fm(bd.d["y_out"], m, 1, 0, TL),
                                             row3(stage[m % 2][:]))
                else:
                    bd.wait("gpsimd", "msV", bd.C["msV"])
                    kw = {} if p == 0 else {"accum_op": ALU.add}
                    for c0 in range(0, TL, 2048):
                        cn = min(2048, TL - c0)
                        stage_free[m % 2] = bd.dma(
                            "gpsimd", tag + "qacc%d" % (m % 2),
                            fm(bd.d["acc_d"], m, 1, c0, cn),
                            row3(stage[m % 2][:, c0:c0 + cn]), **kw)
    if final_merge:
        bd.wait("scalar", tag + "qout0", bd.cur(tag + "qout0"))
        bd.wait("scalar", tag + "qout1", bd.cur(tag + "qout1"))
        bd.wait("gpsimd", tag + "qout0", bd.cur(tag + "qout0"))
        bd.wait("gpsimd", tag + "qout1", bd.cur(tag + "qout1"))
    bd.barrier()
    stx.close()


def emit_attn(bd, cfg):
    DM, TCORE, G, HALO = cfg["DM"], cfg["TCORE"], cfg["G"], cfg["HALO"]
    NF, TCIN = cfg["NF"], cfg["TCIN"]
    GIN, GOWN = cfg["GIN"], cfg["GOWN"]
    stp = contextlib.ExitStack()
    xproj = bd.sb(stp, "axp", [128, NF * TCIN], BF16)
    sty = contextlib.ExitStack()
    yall = bd.sb(sty, "yall", [128, NF * TCIN], BF16)

    with contextlib.ExitStack() as st:
        xin = [bd.sb(st, "a0xin%d" % i, [128, G], F32) for i in range(2)]
        accin = [bd.sb(st, "a0acc%d" % i, [128, G], F32) for i in range(2)]
        xmt = bd.sb(st, "a0xm", [128, NF * G], F32)
        sqt = [bd.sb(st, "a0sq%d" % i, [128, G], F32) for i in range(2)]
        rows = [bd.sb(st, "a0row%d" % i, [1, G], F32) for i in range(4)]
        p_sum = bd.ps(st, "a0p_sum", [1, G])
        p_sumsq = bd.ps(st, "a0p_sumsq", [1, G])
        p_ab = [bd.ps(st, "a0p_ab%d" % i, [128, G]) for i in range(2)]
        bd.drain("sync", ["f1qacc0", "f1qacc1"])
        prev = {}
        free_T = free_V = free_A = 0
        xin_free = [0, 0]
        xmq = 0
        for (t0, nt) in GIN:
            bd.wait("sync", "msT", free_T)
            bd.wait("sync", "msV", free_V)
            bd.wait("sync", "msA", free_A)
            bd.wait("sync", "a0qxm", xmq)
            for f in range(NF):
                fs = f % 2
                bd.wait("sync", "msV", xin_free[fs])
                cx = bd.dma("sync", "a0qx%d" % fs, row3(xin[fs][:, 0:nt]),
                            fm(bd.d["x_in"], f, 1, t0, nt))
                cac = bd.dma("sync", "a0qa%d" % fs, row3(accin[fs][:, 0:nt]),
                             fm(bd.d["acc_d"], f, 1, t0, nt))
                bd.wait("vector", "a0qx%d" % fs, cx)
                bd.wait("vector", "a0qa%d" % fs, cac)
                xin_free[fs] = bd.op(
                    "vector", "scalar_tensor_tensor", xmt[:, f * nt:(f + 1) * nt],
                    accin[fs][:, 0:nt], 0.5, xin[fs][:, 0:nt],
                    ALU.mult, ALU.add, inc="msV")
            cxm = bd.C["msV"]
            bd.wait("scalar", "msV", cxm)
            xmq = bd.dma("scalar", "a0qxm", fm(bd.d["xmid_d"], 0, NF, t0, nt),
                         row3(xmt[:, 0:NF * nt]).rearrange("p o (f t) -> p (o f) t", t=nt))
            bd.wait("tensor", "msV", cxm)
            cbc, cvar = emit_ln_group(
                bd, cfg, lambda f, w=nt: xmt[:, f * w:(f + 1) * w], nt,
                (p_sum, p_sumsq, p_ab, sqt, rows), prev)
            for f in range(NF):
                bd.op("vector", "tensor_tensor", sqt[f % 2][:, 0:nt],
                      xmt[:, f * nt:(f + 1) * nt], p_ab[0][:, 0:nt], ALU.mult)
                bd.op("vector", "tensor_tensor",
                      yall[:, f * TCIN + t0: f * TCIN + t0 + nt],
                      sqt[f % 2][:, 0:nt], p_ab[1][:, 0:nt], ALU.add,
                      inc="msV" if f % 2 == 1 or f == NF - 1 else None)
            apply_V = bd.C["msV"]
            prev = dict(stats_free=cvar, pab_free=apply_V, sq_free=apply_V)
            free_T, free_V, free_A = bd.C["msT"], apply_V, bd.C["msA"]
    bd.barrier()

    ek_written = {}
    sr_written = {}
    with contextlib.ExitStack() as st:
        dsc = bd.sb(st, "adsc", [128, TCIN], F32)
        dsc2 = bd.sb(st, "adsc2", [128, TCIN], F32)
        wsl = [bd.sb(st, "aws%d" % i, [128, NF * 128], F32) for i in range(2)]
        wbf = [bd.sb(st, "awb%d" % i, [128, NF * 128], BF16) for i in range(2)]
        strip = [bd.sb(st, "astr%d" % i, [128, G], F32) for i in range(2)]
        p_mm = [bd.ps(st, "ap_mm%d" % i, [128, G]) for i in range(3)]

        ws_free = [0, 0]
        wb_free = [0, 0]
        strip_free = [0, 0]

        def emit_mix(tm_name, free_at):
            bd.wait("vector", "msT", free_at)
            bd.wait("gpsimd", "msT", free_at)
            tm = bd.t[tm_name]
            for f in range(NF):
                y0 = yall[:, f * TCIN: f * TCIN + TCIN]
                if f % 2 == 0:
                    bd.op("vector", "tensor_sub", dsc[:, 0:TCIN - 1],
                          y0[:, 1:TCIN], y0[:, 0:TCIN - 1])
                    bd.op("vector", "scalar_tensor_tensor",
                          xproj[:, f * TCIN + 1: f * TCIN + TCIN],
                          dsc[:, 0:TCIN - 1], tm[:, f:f + 1], y0[:, 0:TCIN - 1],
                          ALU.mult, ALU.add)
                    bd.op("vector", "tensor_scalar_mul",
                          xproj[:, f * TCIN: f * TCIN + 1], y0[:, 0:1], tm[:, f:f + 1])
                else:
                    bd.op("gpsimd", "tensor_sub", dsc2[:, 0:TCIN - 1],
                          y0[:, 1:TCIN], y0[:, 0:TCIN - 1])
                    bd.op("gpsimd", "tensor_mul", dsc2[:, 0:TCIN - 1],
                          dsc2[:, 0:TCIN - 1],
                          tm[:, f:f + 1].broadcast_to([128, TCIN - 1]))
                    bd.op("gpsimd", "tensor_add",
                          xproj[:, f * TCIN + 1: f * TCIN + TCIN],
                          dsc2[:, 0:TCIN - 1], y0[:, 0:TCIN - 1])
                    bd.op("gpsimd", "tensor_mul",
                          xproj[:, f * TCIN: f * TCIN + 1], y0[:, 0:1], tm[:, f:f + 1])
            return (bd.C["msV"], bd.C["msP"])

        last_mm = 0
        pm_free = [None, None, None]
        for wname, tmname, proj in (("wkT", "tmk_sb", "k"), ("wrT", "tmr_sb", "r")):
            bd.wait("scalar", "msV", bd.C["msV"])
            cmix = emit_mix(tmname, last_mm)
            glist = GIN if proj == "k" else GOWN
            w_ap = bd.d[wname].ap().rearrange("(kc q) d -> q kc d", q=128)

            def load_a(mo):
                sl = mo % 2
                bd.wait("sync", "msA", ws_free[sl])
                cw = bd.dma("sync", "aqw%d" % sl,
                            wsl[sl][:].rearrange("p (kc d) -> p kc d", d=128),
                            w_ap[:, :, mo * 128:(mo + 1) * 128])
                bd.wait("scalar", "aqw%d" % sl, cw)
                bd.wait("scalar", "msT", wb_free[sl])
                cvw = bd.op("scalar", "copy", wbf[sl][:], wsl[sl][:])
                ws_free[sl] = cvw
                return cvw

            pend_w = load_a(0)
            for mo in range(NF):
                sl = mo % 2
                cvw = pend_w
                if mo + 1 < NF:
                    pend_w = load_a(mo + 1)
                for gi, (t0, nt) in enumerate(glist):
                    gp = gi % 3
                    ssl = gi % 2
                    bd.wait("tensor", "msA", cvw)
                    bd.wait("tensor", "msV", cmix[0])
                    bd.wait("tensor", "msP", cmix[1])
                    if pm_free[gp]:
                        bd.wait("tensor", *pm_free[gp])
                    for kc in range(NF):
                        cmm = bd.op("tensor", "matmul", p_mm[gp][:, 0:nt],
                                    wbf[sl][:, kc * 128:(kc + 1) * 128],
                                    xproj[:, kc * TCIN + t0: kc * TCIN + t0 + nt],
                                    start=(kc == 0), stop=(kc == NF - 1),
                                    inc="msT" if kc == NF - 1 else None)
                    if gi == len(glist) - 1:
                        wb_free[sl] = cmm
                    qn = "qstr%d" % ssl
                    bd.wait("scalar", "msT", cmm)
                    bd.wait("scalar", qn, strip_free[ssl])
                    ca = bd.op("scalar", "activation", strip[ssl][:, 0:nt],
                               p_mm[gp][:, 0:nt],
                               AF.Exp if proj == "k" else AF.Sigmoid, inc="msA")
                    pm_free[gp] = ("msA", ca)
                    dname = "ek_d" if proj == "k" else "sr_d"
                    strip_free[ssl] = bd.dma(
                        "scalar", qn, fm(bd.d[dname], mo, 1, t0, nt),
                        row3(strip[ssl][:, 0:nt]))
                    if proj == "k":
                        ek_written[(mo, t0)] = (qn, strip_free[ssl])
                    else:
                        sr_written[(mo, t0)] = (qn, strip_free[ssl])
            last_mm = bd.C["msT"]
        bd.wait("scalar", "msV", bd.C["msV"])
        cmixv = emit_mix("tmv_sb", last_mm)
    sty.close()
    bd.barrier()

    with contextlib.ExitStack() as st:
        rwkv = bd.sb(st, "brw", [128, NF * TCORE], BF16)
        wsl = [bd.sb(st, "vws%d" % i, [128, NF * 128], F32) for i in range(2)]
        wvbf = [bd.sb(st, "vwb%d" % i, [128, NF * 128], BF16) for i in range(2)]
        wobf = [bd.sb(st, "owb%d" % i, [128, NF * 128], BF16) for i in range(2)]
        ekin = [bd.sb(st, "vek%d" % i, [128, G], F32) for i in range(2)]
        srin = [bd.sb(st, "vsr%d" % i, [128, G], F32) for i in range(2)]
        strip = [bd.sb(st, "vstr%d" % i, [128, G], F32) for i in range(2)]
        numo = [bd.sb(st, "bnum%d" % i, [128, G + 1], F32) for i in range(2)]
        deno = [bd.sb(st, "bden%d" % i, [128, G + 1], F32) for i in range(2)]
        bb = [bd.sb(st, "bb%d" % i, [128, G], F32) for i in range(2)]
        bxm = [bd.sb(st, "bxm0", [128, G], F32)] * 2
        bxo = [bd.sb(st, "bxo0", [128, G], F32)] * 2
        p_mm = [bd.ps(st, "bp_mm%d" % i, [128, G]) for i in range(3)]
        p_wo = [bd.ps(st, "bp_wo%d" % i, [128, G]) for i in range(2)]

        w_ap = bd.d["wvT"].ap().rearrange("(kc q) d -> q kc d", q=128)
        wo_ap = bd.d["woT"].ap().rearrange("(kc q) d -> q kc d", q=128)
        wvc_ap = bd.d["wvc"].ap().rearrange("(kc q) d -> q kc d", q=128)
        woc_ap = bd.d["woc"].ap().rearrange("(kc q) d -> q kc d", q=128)

        wsl_free = [None, None]
        wvb_free = [None, None]
        wvb_cache = [None, None]
        wob_free = [0, 0]
        wvc_w = [0] * NF
        woc_w = [0] * NF
        strip_free = [None, None]
        ekin_free = [None, None]
        srin_free = [0, 0]
        pm_free = [None, None, None]
        numo_free = [0, 0]
        crcp_prev = 0
        crw_g = {}
        pm_free2 = [0, 0]
        bxm_free = [0, 0]
        xo_free = [0, 0]

        def load_v(gi, mo):
            sl = mo % 2
            if wvb_free[sl]:
                bd.wait("sync", *wvb_free[sl])
            if wvb_cache[sl]:
                bd.wait("sync", *wvb_cache[sl])
                wvb_cache[sl] = None
            if gi == 0:
                if wsl_free[sl]:
                    bd.wait("sync", *wsl_free[sl])
                cw = bd.dma("sync", "vqw%d" % sl,
                            wsl[sl][:].rearrange("p (kc d) -> p kc d", d=128),
                            w_ap[:, :, mo * 128:(mo + 1) * 128])
                bd.wait("vector", "vqw%d" % sl, cw)
                cvw = bd.op("vector", "tensor_copy", wvbf[sl][:], wsl[sl][:])
                wsl_free[sl] = ("msV", cvw)
                bd.wait("scalar", "msV", cvw)
                wvc_w[mo] = ("qwvc%d" % sl, bd.dma(
                    "scalar", "qwvc%d" % sl,
                    wvc_ap[:, :, mo * 128:(mo + 1) * 128],
                    wvbf[sl][:].rearrange("p (kc d) -> p kc d", d=128)))
                wvb_cache[sl] = wvc_w[mo]
                return ("msV", cvw)
            bd.wait("sync", *wvc_w[mo])
            cw = bd.dma("sync", "vqw%d" % sl,
                        wvbf[sl][:].rearrange("p (kc d) -> p kc d", d=128),
                        wvc_ap[:, :, mo * 128:(mo + 1) * 128])
            return ("vqw%d" % sl, cw)

        def emit_wo(gi):
            t0o = (gi - 1) * G
            bd.wait("tensor", "msV", crw_g[gi][0])
            bd.wait("tensor", "msP", crw_g[gi][1])
            for co in range(NF):
                sl = co % 2
                if wob_free[sl]:
                    bd.wait("sync", *wob_free[sl])
                bd.wait("sync", *woc_w[co])
                cwo = bd.dma("sync", "oqw%d" % sl,
                             wobf[sl][:].rearrange("p (kc d) -> p kc d", d=128),
                             woc_ap[:, :, co * 128:(co + 1) * 128])
                gp2 = co % 2
                xsl = 0
                bd.wait("sync", "msP", bxm_free[xsl])
                cxm = bd.dma("sync", "bqxm%d" % xsl, row3(bxm[xsl][:]),
                             fm(bd.d["xmid_d"], co, 1, t0o + HALO, G))
                bd.wait("tensor", "oqw%d" % sl, cwo)
                bd.wait("tensor", "msA", pm_free2[gp2])
                for kc in range(NF):
                    cmm = bd.op("tensor", "matmul", p_wo[gp2][:],
                                wobf[sl][:, kc * 128:(kc + 1) * 128],
                                rwkv[:, kc * TCORE + t0o: kc * TCORE + t0o + G],
                                start=(kc == 0), stop=(kc == NF - 1),
                                inc="msT" if kc == NF - 1 else None)
                wob_free[sl] = ("msT", cmm)
                bd.wait("scalar", "msT", cmm)
                bd.wait("scalar", "bqxa%d" % xsl, xo_free[xsl])
                bd.wait("scalar", "msP", bxm_free[xsl])
                ca = bd.op("scalar", "copy", bxo[xsl][:], p_wo[gp2][:])
                pm_free2[gp2] = ca
                bd.wait("gpsimd", "msA", ca)
                bd.wait("gpsimd", "bqxm%d" % xsl, cxm)
                cp = bd.op("gpsimd", "tensor_tensor", bxo[xsl][:], bxo[xsl][:],
                           bxm[xsl][:], ALU.add)
                bxm_free[xsl] = cp
                bd.wait("scalar", "msP", cp)
                xo_free[xsl] = bd.dma("scalar", "bqxa%d" % xsl,
                                      fm(bd.d["xattn_d"], co, 1, t0o, G),
                                      row3(bxo[xsl][:]))

        for co in range(NF):
            sl = co % 2
            if wsl_free[sl]:
                bd.wait("sync", *wsl_free[sl])
            cw = bd.dma("sync", "oqwl%d" % sl,
                        wsl[sl][:].rearrange("p (kc d) -> p kc d", d=128),
                        wo_ap[:, :, co * 128:(co + 1) * 128])
            bd.wait("scalar", "oqwl%d" % sl, cw)
            if wob_free[sl]:
                bd.wait("scalar", *wob_free[sl])
            cvo = bd.op("scalar", "copy", wobf[sl][:], wsl[sl][:])
            wsl_free[sl] = ("msA", cvo)
            bd.wait("scalar", "msA", cvo)
            woc_w[co] = ("qwoc%d" % sl, bd.dma(
                "scalar", "qwoc%d" % sl,
                woc_ap[:, :, co * 128:(co + 1) * 128],
                wobf[sl][:].rearrange("p (kc d) -> p kc d", d=128)))
            wob_free[sl] = woc_w[co]

        seq = [(gi, mo) for gi in range(len(GIN)) for mo in range(NF)]
        pend_v = load_v(*seq[0])
        for gi, (t0, nt) in enumerate(GIN):
            for mo in range(NF):
                sl = mo % 2
                cvw = pend_v
                idx = gi * NF + mo
                if idx + 1 < len(seq):
                    pend_v = load_v(*seq[idx + 1])
                gp = (gi * NF + mo) % 3
                bd.wait("tensor", *cvw)
                bd.wait("tensor", "msV", cmixv[0])
                bd.wait("tensor", "msP", cmixv[1])
                if pm_free[gp]:
                    bd.wait("tensor", *pm_free[gp])
                for kc in range(NF):
                    cmm = bd.op("tensor", "matmul", p_mm[gp][:, 0:nt],
                                wvbf[sl][:, kc * 128:(kc + 1) * 128],
                                xproj[:, kc * TCIN + t0: kc * TCIN + t0 + nt],
                                start=(kc == 0), stop=(kc == NF - 1),
                                inc="msT" if kc == NF - 1 else None)
                wvb_free[sl] = ("msT", cmm)
                ssl = mo % 2
                eq, ec = ek_written[(mo, t0)]
                bd.wait("sync", eq, ec)
                if ekin_free[ssl]:
                    bd.wait("sync", *ekin_free[ssl])
                cek = bd.dma("sync", "vqe%d" % ssl, row3(ekin[ssl][:, 0:nt]),
                             fm(bd.d["ek_d"], mo, 1, t0, nt))
                bd.wait("vector", "vqe%d" % ssl, cek)
                bd.wait("vector", "msT", cmm)
                if strip_free[ssl]:
                    bd.wait("vector", *strip_free[ssl])
                cv = bd.op("vector", "tensor_tensor", strip[ssl][:, 0:nt],
                           p_mm[gp][:, 0:nt], ekin[ssl][:, 0:nt], ALU.mult)
                pm_free[gp] = ("msV", cv)
                cer = bd.op("vector", "tensor_scalar_mul", bd.t["ewrep"][:, 0:nt],
                            bd.t["ones_g"][:, 0:nt], bd.t["ew"][:, mo:mo + 1])
                if gi == 0:
                    bd.op("vector", "tensor_scalar_mul", bb[0][:, 0:nt],
                          strip[ssl][:, 0:nt], bd.t["hv_sb"][:, 0:1])
                    bd.op("vector", "tensor_tensor_scan", bb[0][:, 0:nt],
                          bd.t["ewrep"][:, 0:nt], bb[0][:, 0:nt],
                          0.0, ALU.mult, ALU.add)
                    bd.op("vector", "tensor_copy",
                          bd.t["numcar"][:, mo:mo + 1], bb[0][:, nt - 1:nt])
                    bd.op("vector", "tensor_scalar_mul", bb[0][:, 0:nt],
                          ekin[ssl][:, 0:nt], bd.t["hv_sb"][:, 0:1])
                    cds = bd.op("vector", "tensor_tensor_scan", bb[0][:, 0:nt],
                                bd.t["ewrep"][:, 0:nt], bb[0][:, 0:nt],
                                0.0, ALU.mult, ALU.add)
                    ekin_free[ssl] = ("msV", cds)
                    strip_free[ssl] = ("msV", cds)
                    cfin = bd.op("vector", "tensor_copy",
                                 bd.t["dencar"][:, mo:mo + 1], bb[0][:, nt - 1:nt])
                    continue
                sq, sc2 = sr_written[(mo, t0)]
                bd.wait("sync", sq, sc2)
                bd.wait("sync", "msV", srin_free[ssl])
                csr = bd.dma("sync", "vqs%d" % ssl, row3(srin[ssl][:]),
                             fm(bd.d["sr_d"], mo, 1, t0, nt))
                pp = idx % 2
                no, de = numo[pp], deno[pp]
                bd.wait("vector", "msP", numo_free[pp])
                bd.op("vector", "tensor_copy", no[:, 0:1],
                      bd.t["numcar"][:, mo:mo + 1])
                cns = bd.op("vector", "tensor_tensor_scan", no[:, 1:G + 1],
                            bd.t["ewrep"][:], strip[ssl][:], no[:, 0:1],
                            ALU.mult, ALU.add)
                bd.op("vector", "tensor_copy", bd.t["numcar"][:, mo:mo + 1],
                      no[:, G:G + 1])
                bd.op("vector", "tensor_copy", de[:, 0:1],
                      bd.t["dencar"][:, mo:mo + 1])
                cds = bd.op("vector", "tensor_tensor_scan", de[:, 1:G + 1],
                            bd.t["ewrep"][:], ekin[ssl][:], de[:, 0:1],
                            ALU.mult, ALU.add)
                bd.op("vector", "tensor_copy", bd.t["dencar"][:, mo:mo + 1],
                      de[:, G:G + 1])
                bd.wait("gpsimd", "msV", cds)
                bd.wait("gpsimd", "msA", crcp_prev)
                bd.op("gpsimd", "tensor_mul", bb[0][:], ekin[ssl][:],
                      bd.t["eu"][:, mo:mo + 1].broadcast_to([128, G]))
                cp1 = bd.op("gpsimd", "tensor_mul", bb[1][:], strip[ssl][:],
                            bd.t["eu"][:, mo:mo + 1].broadcast_to([128, G]))
                ekin_free[ssl] = ("msP", cp1)
                strip_free[ssl] = ("msP", cp1)
                bd.op("gpsimd", "tensor_add", no[:, 0:G], no[:, 0:G], bb[1][:])
                cden = bd.op("gpsimd", "tensor_add", de[:, 0:G], de[:, 0:G],
                             bb[0][:])
                bd.wait("scalar", "msP", cden)
                bd.op("scalar", "activation", bb[1][:], de[:, 0:G], AF.Ln)
                crcp = bd.op("scalar", "activation", bb[1][:], bb[1][:], AF.Exp,
                             scale=-1.0)
                crcp_prev = crcp
                bd.wait("vector", "msA", crcp)
                bd.op("vector", "tensor_tensor", no[:, 0:G], no[:, 0:G], bb[1][:],
                      ALU.mult)
                bd.wait("vector", "vqs%d" % ssl, csr)
                cvo2 = bd.op(
                    "vector", "tensor_tensor",
                    rwkv[:, mo * TCORE + t0 - HALO: mo * TCORE + t0 - HALO + nt],
                    no[:, 0:G], srin[ssl][:], ALU.mult)
                srin_free[ssl] = cvo2
                numo_free[pp] = cden
            if gi >= 1:
                crw_g[gi] = (bd.C["msV"], bd.C["msP"])
                if gi >= 2:
                    emit_wo(gi - 1)
        emit_wo(len(GIN) - 1)
    stp.close()
    bd.barrier()
def emit_out(bd, cfg):
    TCORE, G, NF = cfg["TCORE"], cfg["G"], cfg["NF"]
    with contextlib.ExitStack() as st:
        xin = bd.sb(st, "oxin", [128, NF * G], F32)
        accin = bd.sb(st, "oacc", [128, NF * G], F32)
        outt = bd.sb(st, "outt", [128, NF * G], F32)
        bd.drain("sync", ["f2qacc0", "f2qacc1", "bqxa0", "bqxa1"])
        free_V = 0
        out_done = 0
        for (t0, nt) in cfg["G0"]:
            bd.wait("sync", "msV", free_V)
            cx = bd.dma("sync", "oqx", row3(xin[:]).rearrange(
                "p o (f t) -> p (o f) t", t=nt), fm(bd.d["xattn_d"], 0, NF, t0, nt))
            cac = bd.dma("sync", "oqa", row3(accin[:]).rearrange(
                "p o (f t) -> p (o f) t", t=nt), fm(bd.d["acc_d"], 0, NF, t0, nt))
            bd.wait("vector", "oqx", cx)
            bd.wait("vector", "oqa", cac)
            bd.wait("vector", "oqout", out_done)
            for f in range(NF):
                bd.op("vector", "scalar_tensor_tensor", outt[:, f * nt:(f + 1) * nt],
                      accin[:, f * nt:(f + 1) * nt], 0.5, xin[:, f * nt:(f + 1) * nt],
                      ALU.mult, ALU.add, inc="msV" if f == NF - 1 else None)
            free_V = bd.C["msV"]
            bd.wait("scalar", "msV", free_V)
            out_done = bd.dma("scalar", "oqout", fm(bd.d["y_out"], 0, NF, t0, nt),
                              row3(outt[:]).rearrange("p o (f t) -> p (o f) t", t=nt))
        bd.wait("scalar", "oqout", out_done)
        bd.wait("gpsimd", "oqout", out_done)


def build_program(cfg):
    cfg = derive(cfg)
    DM, H, E, TCORE, G = cfg["DM"], cfg["H"], cfg["E"], cfg["TCORE"], cfg["G"]
    NF, TCIN = cfg["NF"], cfg["TCIN"]

    nc = bass.Bass("TRN2", target_bir_lowering=False)
    d = {}
    d["x_in"] = nc.dram_tensor("x", [NF, 128, TCIN], F32, kind="ExternalInput")
    d["wg"] = nc.dram_tensor("wg", [E, DM, H], F32, kind="ExternalInput")
    d["wu"] = nc.dram_tensor("wu", [E, DM, H], F32, kind="ExternalInput")
    d["wd"] = nc.dram_tensor("wd", [E, H, DM], F32, kind="ExternalInput")
    d["sg"] = nc.dram_tensor("sg", [DM, 2 * H], F32, kind="ExternalInput")
    d["su"] = nc.dram_tensor("su", [DM, 2 * H], F32, kind="ExternalInput")
    d["sd"] = nc.dram_tensor("sd", [2 * H, DM], F32, kind="ExternalInput")
    for n in ("wkT", "wvT", "wrT", "woT"):
        d[n] = nc.dram_tensor(n, [DM, DM], F32, kind="ExternalInput")
    d["routT"] = nc.dram_tensor("routT", [128, NF * E], F32, kind="ExternalInput")
    for n in ("td", "tf", "tmk", "tmv", "tmr"):
        d[n] = nc.dram_tensor(n, [128, NF], F32, kind="ExternalInput")
    d["hv"] = nc.dram_tensor("hv", [128, 1], F32, kind="ExternalInput")
    d["ident"] = nc.dram_tensor("ident", [128, 128], F32, kind="ExternalInput")
    d["y_out"] = nc.dram_tensor("y", [NF, 128, TCORE], F32, kind="ExternalOutput")
    dbg = "ExternalOutput" if DEBUG_INTERNALS else None
    for n in ("acc_d", "xmid_d", "ek_d", "ekv_d", "sr_d"):
        d[n] = nc.dram_tensor(n, [NF, 128, TCIN], F32, kind=dbg) if dbg else \
            nc.dram_tensor(n, [NF, 128, TCIN], F32)
    d["xattn_d"] = nc.dram_tensor("xattn_d", [NF, 128, TCORE], F32, kind=dbg) if dbg \
        else nc.dram_tensor("xattn_d", [NF, 128, TCORE], F32)
    d["wvc"] = nc.dram_tensor("wvc", [DM, DM], BF16)
    d["woc"] = nc.dram_tensor("woc", [DM, DM], BF16)
    d["gates_d"] = nc.dram_tensor("gates_d", [len(cfg["GIN"]) * E, G], F32)

    with contextlib.ExitStack() as st:
        bd = B(nc, st, cfg)
        bd.d = d
        t = bd.t
        for n, sh in [("ones_col", [128, 1]), ("ones_row", [1, 128]),
                      ("ones_g", [128, G]), ("ident_sb", [128, 128]),
                      ("ew", [128, NF]), ("eu", [128, NF]), ("ewrep", [128, G]),
                      ("tmk_sb", [128, NF]), ("tmv_sb", [128, NF]), ("tmr_sb", [128, NF]),
                      ("tdsb", [128, NF]), ("tfsb", [128, NF]), ("hv_sb", [128, 1]),
                      ("routT_sb", [128, NF * E]),
                      ("numcar", [128, NF]), ("dencar", [128, NF])]:
            t[n] = bd.sb(st, n, sh, F32)

        for n, dst in [("td", "tdsb"), ("tf", "tfsb"), ("tmk", "tmk_sb"),
                       ("tmv", "tmv_sb"), ("tmr", "tmr_sb"), ("ident", "ident_sb"),
                       ("hv", "hv_sb"), ("routT", "routT_sb")]:
            ci = bd.dma("sync", "qinit", t[dst][:], d[n].ap())
        bd.op("vector", "memset", t["ones_col"][:], 1.0)
        bd.op("vector", "memset", t["ones_row"][:], 1.0)
        bd.op("vector", "memset", t["ones_g"][:], 1.0)
        bd.op("vector", "memset", t["numcar"][:], 0.0)
        cv = bd.op("vector", "memset", t["dencar"][:], 0.0, inc="msV")
        bd.wait("scalar", "qinit", ci)
        bd.op("scalar", "activation", t["ew"][:], t["tdsb"][:], AF.Exp)
        bd.op("scalar", "activation", t["ew"][:], t["ew"][:], AF.Exp, scale=-1.0)
        ca = bd.op("scalar", "activation", t["eu"][:], t["tfsb"][:], AF.Exp, inc="msA")
        for eng in B.ENGS:
            bd.wait(eng, "qinit", ci)
            bd.wait(eng, "msV", cv)
            bd.wait(eng, "msA", ca)

        emit_ffn(bd, cfg, d["x_in"], "f1", cfg["GIN"], TCIN)
        emit_attn(bd, cfg)
        emit_ffn(bd, cfg, d["xattn_d"], "f2", cfg["G0"], TCORE,
                 src_ready_q=["bqxa0", "bqxa1"], final_merge=True)

        block = st.enter_context(nc.Block())

        @block.tensor
        def _(e):
            for fn in bd.q["tensor"]:
                fn(e)

        @block.vector
        def _(e):
            for fn in bd.q["vector"]:
                fn(e)

        @block.scalar
        def _(e):
            for fn in bd.q["scalar"]:
                fn(e)

        @block.gpsimd
        def _(e):
            for fn in bd.q["gpsimd"]:
                fn(e)

        @block.sync
        def _(e):
            for fn in bd.q["sync"]:
                fn(e)

    return nc


def make_in_maps(cfg, inputs):
    cfg = derive(cfg)
    DM, H, E, TCORE = cfg["DM"], cfg["H"], cfg["E"], cfg["TCORE"]
    NF, TCIN, HALO = cfg["NF"], cfg["TCIN"], cfg["HALO"]
    CORES, CPB = 8, 4

    x = np.ascontiguousarray(np.asarray(inputs["L0_attn_x"], dtype=np.float32))
    Bb, T, C = x.shape
    x_fm = np.ascontiguousarray(np.transpose(x, (0, 2, 1)))

    def chan_tile(v):
        return np.ascontiguousarray(
            np.asarray(v, np.float32).reshape(-1)[0:DM].reshape(NF, 128).T)

    rt = np.asarray(inputs["L0_ffn_router_w"], np.float32)
    routT = np.ascontiguousarray(
        rt.T.reshape(NF, 128, E).transpose(1, 0, 2).reshape(128, NF * E))

    base = dict(
        wg=np.ascontiguousarray(np.asarray(inputs["L0_ffn_wg"], np.float32)),
        wu=np.ascontiguousarray(np.asarray(inputs["L0_ffn_wu"], np.float32)),
        wd=np.ascontiguousarray(np.asarray(inputs["L0_ffn_wd"], np.float32)),
        sg=np.ascontiguousarray(np.asarray(inputs["L0_ffn_sg"], np.float32)),
        su=np.ascontiguousarray(np.asarray(inputs["L0_ffn_su"], np.float32)),
        sd=np.ascontiguousarray(np.asarray(inputs["L0_ffn_sd"], np.float32)),
        wkT=np.ascontiguousarray(np.asarray(inputs["L0_attn_Wk"], np.float32).T),
        wvT=np.ascontiguousarray(np.asarray(inputs["L0_attn_Wv"], np.float32).T),
        wrT=np.ascontiguousarray(np.asarray(inputs["L0_attn_Wr"], np.float32).T),
        woT=np.ascontiguousarray(np.asarray(inputs["L0_attn_Wo"], np.float32).T),
        routT=routT,
        td=chan_tile(inputs["L0_attn_time_decay"]),
        tf=chan_tile(inputs["L0_attn_time_first"]),
        tmk=chan_tile(inputs["L0_attn_time_mix_k"]),
        tmv=chan_tile(inputs["L0_attn_time_mix_v"]),
        tmr=chan_tile(inputs["L0_attn_time_mix_r"]),
        ident=np.eye(128, dtype=np.float32),
    )
    maps = []
    for c in range(CORES):
        b, i = c // CPB, c % CPB
        m = dict(base)
        xs = np.zeros((C, TCIN), np.float32)
        lo = i * TCORE - HALO
        src_lo = max(lo, 0)
        xs[:, src_lo - lo:] = x_fm[b][:, src_lo: (i + 1) * TCORE]
        m["x"] = np.ascontiguousarray(xs.reshape(NF, 128, TCIN))
        m["hv"] = np.full((128, 1), 0.0 if i == 0 else 1.0, np.float32)
        maps.append(m)
    return maps


def assemble_out(cfg, results, B_, T, C):
    cfg = derive(cfg)
    TCORE = cfg["TCORE"]
    CPB = 4
    out_fm = np.empty((B_, C, T), np.float32)
    for c in range(8):
        b, i = c // CPB, c % CPB
        out_fm[b][:, i * TCORE:(i + 1) * TCORE] = \
            np.asarray(results[c]["y"]).reshape(C, TCORE)
    return np.ascontiguousarray(np.transpose(out_fm, (0, 2, 1)))


def kernel(**inputs):
    cfg = FULL_CFG
    x = np.asarray(inputs["L0_attn_x"])
    B_, T, C = x.shape
    nc = build_program(cfg)
    maps = make_in_maps(cfg, inputs)
    res = run_bass_kernel_spmd(nc, maps, core_ids=list(range(8)))
    return assemble_out(cfg, res.results, B_, T, C)
